# version 17
# baseline (speedup 1.0000x reference)
import numpy as np
from contextlib import ExitStack

import concourse.bass as bass
import concourse.mybir as mybir
from concourse.bass_utils import run_bass_kernel_spmd

F32 = mybir.dt.float32
BF16 = mybir.dt.bfloat16
I32 = mybir.dt.int32
AF = mybir.ActivationFunctionType
ALU = mybir.AluOpType
AX = mybir.AxisListType

SPARSE = True

P = 128
D = 1024
DC = D // P
F = 4096
FC = F // P
E = 8
TT = 512
PCT = TT // P
EPS = 1e-5


class Slot:
    __slots__ = ("w", "r")

    def __init__(self):
        self.w = None
        self.r = []


class Sync:

    def __init__(self, nc, stack):
        self.nc = nc
        self.stack = stack
        self.sems = {}
        self.cnt = {}
        for name in ("pe", "act", "dve", "pool", "sp"):
            self._mk(name)
        self.last_wait = {}

    def _mk(self, name):
        self.sems[name] = self.stack.enter_context(self.nc.semaphore("S_" + name))
        self.cnt[name] = 0

    def stream(self, name):
        key = "q_" + name
        if key not in self.sems:
            self._mk(key)
        return key

    def done(self, inst, eng):
        self.cnt[eng] += 1
        inst.then_inc(self.sems[eng], 1)
        return (eng, self.cnt[eng])

    def dma_done(self, inst, stream):
        key = "q_" + stream
        self.cnt[key] += 16
        inst.then_inc(self.sems[key], 16)
        return (key, self.cnt[key])

    def wait(self, eng_obj, waiter, *toks):
        need = {}
        for t in toks:
            if t is None:
                continue
            s, v = t
            if v > need.get(s, -1):
                need[s] = v
        for s, v in sorted(need.items()):
            key = (waiter, s)
            if self.last_wait.get(key, -1) >= v:
                continue
            eng_obj.wait_ge(self.sems[s], v)
            self.last_wait[key] = v

    def bw(self, eng_obj, waiter, *slots):
        toks = []
        for s in slots:
            if s.w is not None:
                toks.append(s.w)
            toks.extend(s.r)
        self.wait(eng_obj, waiter, *toks)

    def br(self, eng_obj, waiter, *slots):
        self.wait(eng_obj, waiter, *[s.w for s in slots if s.w is not None])

    @staticmethod
    def mw(tok, *slots):
        for s in slots:
            s.w = tok
            s.r = []

    @staticmethod
    def mr(tok, *slots):
        for s in slots:
            s.r.append(tok)

    def barrier(self, block, streams=()):
        toks = [(k, v) for k, v in self.cnt.items() if v > 0]
        for name, opener in (("pe", block.tensor), ("act", block.scalar),
                             ("dve", block.vector), ("pool", block.gpsimd),
                             ("sp", block.sync)):
            def f(eng, name=name):
                self.wait(eng, name, *toks)
            opener(f)


def build(T=8192):
    NT = T // TT
    NPC = T // P
    nc = bass.Bass("TRN2", target_bir_lowering=False)

    dt_in = {
        "x": ([T, D], F32), "WiT": ([D, D], F32), "WgT": ([D, D], F32),
        "WoT": ([D, D], F32), "bi": ([P, DC], F32), "bg": ([P, DC], F32),
        "bo": ([P, DC], F32), "W1": ([E, D, F], F32), "b1t": ([E, P, FC], F32),
        "W2": ([E, F, D], F32), "b2t": ([E, P, DC], F32),
        "Wrt": ([P, DC, E], F32), "br": ([1, E], F32),
        "gamma": ([1, D], F32), "beta": ([1, D], F32),
    }
    dh = {k: nc.dram_tensor(k, sh, dt, kind="ExternalInput")
          for k, (sh, dt) in dt_in.items()}
    d = {k: h.ap() for k, h in dh.items()}
    x1s = nc.dram_tensor("x1s", [T, D], F32, kind="Internal").ap()
    ln3s = nc.dram_tensor("ln3s", [DC, P, T], BF16, kind="Internal").ap()
    accs = nc.dram_tensor("accs", [DC, P, T], F32, kind="Internal").ap()
    out_d = nc.dram_tensor("out", [T, D], F32, kind="ExternalOutput").ap()

    st = ExitStack()
    with st:
        S = Sync(nc, st)
        for q in ("cst", "x1w0", "x1w1", "x1w2", "x1w3", "ln3w", "ln3r",
                  "w1", "w2", "bld", "accw", "acc0", "acc1", "x1c0", "x1c1",
                  "x1c2", "outw0", "outw1"):
            S.stream(q)
        for i in range(8):
            S.stream(f"xt{i}")
        block = st.enter_context(nc.Block())

        def sb(name, shape, dt=F32):
            return st.enter_context(nc.sbuf_tensor(name, shape, dt))

        identb = sb("identb", [P, P], BF16)
        identf = sb("identf", [P, P], F32)
        eps_t = sb("eps_t", [P, 1], F32)
        ones1f = sb("ones1f", [1, P], F32)
        bi_s = sb("bi_s", [P, DC]); bg_s = sb("bg_s", [P, DC])
        bgn_s = sb("bgn_s", [P, DC]); bo_s = sb("bo_s", [P, DC])
        wrt_s = sb("wrt_s", [P, DC * E]); br_s = sb("br_s", [1, E])
        carry = sb("carry", [P, DC])
        rsum = sb("rsum", [P, DC])
        m3_all = sb("m3_all", [P, NPC]); rs3_all = sb("rs3_all", [P, NPC])
        cwb = sb("cwb", [P, E])

        TK = {}

        def sync_startup(sync):
            TK["bi"] = S.dma_done(sync.dma_start(out=bi_s[:], in_=d["bi"][:]), "cst")
            S.dma_done(sync.dma_start(out=bg_s[:], in_=d["bg"][:]), "cst")
            S.dma_done(sync.dma_start(out=bo_s[:], in_=d["bo"][:]), "cst")
            S.dma_done(sync.dma_start(
                out=wrt_s[:].rearrange("p (dc e) -> p dc e", e=E),
                in_=d["Wrt"][:]), "cst")
            TK["cst"] = S.dma_done(sync.dma_start(out=br_s[:], in_=d["br"][:]), "cst")
        block.sync(sync_startup)

        with nc.sbuf_tensor("io_r", [P, P], I32) as io_r, \
             nc.sbuf_tensor("io_c", [P, 1], I32) as io_c, \
             nc.sbuf_tensor("io_rf", [P, P], F32) as io_rf, \
             nc.sbuf_tensor("io_cf", [P, 1], F32) as io_cf:

            def g_start(g):
                TK["ior"] = S.done(g.iota(io_r[:], [[1, P]], channel_multiplier=0), "pool")
                TK["ioc"] = S.done(g.iota(io_c[:], [[0, 1]], channel_multiplier=1), "pool")
            block.gpsimd(g_start)

            def v_start(v):
                S.wait(v, "dve", TK["ior"], TK["ioc"])
                t1 = S.done(v.tensor_copy(io_rf[:], io_r[:]), "dve")
                t2 = S.done(v.tensor_copy(io_cf[:], io_c[:]), "dve")
                S.wait(v, "dve", t1, t2)
                t3 = S.done(v.tensor_scalar(
                    identf[:], io_rf[:], io_cf[:], None, ALU.is_equal), "dve")
                S.wait(v, "dve", t3)
                TK["idb"] = S.done(v.tensor_copy(identb[:], identf[:]), "dve")
                TK["idf"] = t3
                S.done(v.memset(eps_t[:], EPS), "dve")
                S.done(v.memset(ones1f[:], 1.0), "dve")
                S.done(v.memset(carry[:], 0.0), "dve")
                TK["z"] = S.done(v.memset(rsum[:], 0.0), "dve")
                S.wait(v, "dve", TK["cst"], TK["bi"])
                TK["bgn"] = S.done(v.tensor_scalar(
                    bgn_s[:], bg_s[:], -1.0, None, ALU.mult), "dve")
            block.vector(v_start)

        S.barrier(block)

        with ExitStack() as stA:
            def sba(name, shape, dt=F32):
                return stA.enter_context(nc.sbuf_tensor(name, shape, dt))
            NXT = 12
            xt = sba("xt", [P, NXT * D])
            wiT = sba("wiT", [P, D * DC], BF16)
            wgT = sba("wgT", [P, D * DC], BF16)
            woT = sba("woT", [P, D * DC], BF16)
            ln1b = sba("ln1b", [P, 8 * D], BF16)
            ln1f = sba("ln1f", [P, DC * TT], BF16)
            NUG = 4
            uB = sba("uB", [P, NUG * TT]); gB = sba("gB", [P, NUG * TT])
            gtmp = sba("gtmp", [P, TT])
            h32 = sba("h32", [P, 2 * TT])
            hbf = sba("hbf", [P, 2 * DC * TT], BF16)
            aof = sba("aof", [P, DC * TT])
            aot = sba("aot", [P, 4 * D])
            ln3b = sba("ln3b", [P, 4 * D], BF16)
            ln3f = sba("ln3f", [P, 4 * TT], BF16)
            st6 = sba("st6", [P, 12 * 12])
            mv = sba("mv", [P, 12 * 2])
            sdv = sba("sdv", [P, 12]); rstd = sba("rstd", [P, 12])
            nmr = sba("nmr", [P, 12])
            tmp1 = sba("tmp1", [P, DC])

            psA = [stA.enter_context(nc.psum_tensor(f"psA{i}", [P, TT], F32))
                   for i in range(4)]
            psTB = [stA.enter_context(nc.psum_tensor(f"psTB{i}", [P, 8 * P], BF16))
                    for i in range(2)]
            psTF = [stA.enter_context(nc.psum_tensor(f"psTF{i}", [P, 4 * P], F32))
                    for i in range(2)]

            s_xt = [Slot() for _ in range(NXT)]
            s_ln1b = [Slot() for _ in range(8)]
            s_ln1f = [Slot() for _ in range(DC)]
            s_u = [Slot() for _ in range(NUG)]
            s_g = [Slot() for _ in range(NUG)]
            s_gtmp = [Slot()]
            s_h32 = [Slot() for _ in range(2)]
            s_hbf = [Slot() for _ in range(2)]
            s_aof = [Slot() for _ in range(DC)]
            s_aot = [Slot() for _ in range(4)]
            s_ln3b = [Slot() for _ in range(4)]
            s_ln3f = Slot()
            s_psA = [Slot() for _ in range(4)]
            s_psTB = [Slot() for _ in range(2)]
            s_psTF = [Slot() for _ in range(2)]
            s_carry = [Slot() for _ in range(DC)]
            s_stats = [Slot() for _ in range(12)]
            s_rsum = Slot()
            s_tmp1 = [Slot() for _ in range(2)]

            def g_wload(g):
                S.dma_done(g.dma_start(
                    out=wiT[:].rearrange("p (dc e) -> p dc e", e=D),
                    in_=d["WiT"].rearrange("(dc p) e -> p dc e", p=P)), "w1")
                S.dma_done(g.dma_start(
                    out=wgT[:].rearrange("p (dc e) -> p dc e", e=D),
                    in_=d["WgT"].rearrange("(dc p) e -> p dc e", p=P)), "w1")
                tok = S.dma_done(g.dma_start(
                    out=woT[:].rearrange("p (dc e) -> p dc e", e=D),
                    in_=d["WoT"].rearrange("(dc p) e -> p dc e", p=P)), "w1")
                TK["wiT"] = TK["wgT"] = TK["woT"] = tok
            block.gpsimd(g_wload)

            def ln_chain(v_or_none, tc, src_ap, want_bias=True):
                pass

            def emit_ln_stats(tile_tag, tc, src_ap, src_slots):
                sl = s_stats[tc]

                def v1(v):
                    S.bw(v, "dve", sl)
                    S.br(v, "dve", *src_slots)
                    v.bn_stats(st6[:, tc * 12:tc * 12 + 6], src_ap[:, 0:512])
                    t = S.done(v.bn_stats(
                        st6[:, tc * 12 + 6:tc * 12 + 12], src_ap[:, 512:1024]), "dve")
                    S.wait(v, "dve", t)
                    t2 = S.done(v.bn_aggr(
                        mv[:, tc * 2:tc * 2 + 2], st6[:, tc * 12:(tc + 1) * 12]), "dve")
                    TK[(tile_tag, "mv", tc)] = t2
                block.vector(v1)

                def a1(a):
                    S.wait(a, "act", TK[(tile_tag, "mv", tc)])
                    TK[(tile_tag, "sd", tc)] = S.done(a.activation(
                        sdv[:, tc:tc + 1], mv[:, tc * 2 + 1:tc * 2 + 2],
                        AF.Sqrt, bias=eps_t[:]), "act")
                block.scalar(a1)

                def v2(v):
                    S.wait(v, "dve", TK[(tile_tag, "sd", tc)])
                    t2 = S.done(v.reciprocal(rstd[:, tc:tc + 1],
                                             sdv[:, tc:tc + 1]), "dve")
                    S.mw(t2, sl)
                    for ss in src_slots:
                        S.mr(t2, ss)
                    TK[(tile_tag, "ln", tc)] = t2
                block.vector(v2)

            def emit_ln_batch(tag, specs, grp):
                def v1(v):
                    for tc, src_ap, slots in specs:
                        q = grp * 4 + tc
                        S.bw(v, "dve", s_stats[q])
                        S.br(v, "dve", *slots)
                        v.bn_stats(st6[:, q * 12:q * 12 + 6], src_ap[:, 0:512])
                        t = S.done(v.bn_stats(
                            st6[:, q * 12 + 6:q * 12 + 12],
                            src_ap[:, 512:1024]), "dve")
                        S.wait(v, "dve", t)
                        TK[(tag, "mv", tc)] = S.done(v.bn_aggr(
                            mv[:, q * 2:q * 2 + 2],
                            st6[:, q * 12:(q + 1) * 12]), "dve")
                block.vector(v1)

                def a1(a):
                    S.wait(a, "act",
                           *[TK[(tag, "mv", tc)] for tc, _, _ in specs])
                    for tc, _, _ in specs:
                        q = grp * 4 + tc
                        tok = S.done(a.activation(
                            sdv[:, q:q + 1], mv[:, q * 2 + 1:q * 2 + 2],
                            AF.Sqrt, bias=eps_t[:]), "act")
                    TK[(tag, "sd")] = tok
                block.scalar(a1)

                def v2(v):
                    S.wait(v, "dve", TK[(tag, "sd")])
                    for tc, _, slots in specs:
                        q = grp * 4 + tc
                        t1 = S.done(v.reciprocal(
                            rstd[:, q:q + 1], sdv[:, q:q + 1]), "dve")
                        S.wait(v, "dve", t1)
                        tok = S.done(v.tensor_scalar(
                            nmr[:, q:q + 1], mv[:, q * 2:q * 2 + 1],
                            rstd[:, q:q + 1], -1.0, ALU.mult, ALU.mult), "dve")
                        S.mw(tok, s_stats[q])
                        for ss in slots:
                            S.mr(tok, ss)
                        TK[(tag, "ln", tc)] = tok
                block.vector(v2)

            def stage1a(i):
                t0 = i * TT
                lb = (i % 2) * PCT

                def q1(sync):
                    for tc in range(PCT):
                        k = (i * PCT + tc) % NXT
                        S.bw(sync, "sp", s_xt[k])
                        tok = S.dma_done(sync.dma_start(
                            out=xt[:, k * D:(k + 1) * D],
                            in_=d["x"][t0 + tc * P: t0 + (tc + 1) * P, :]),
                            f"xt{k % 8}")
                        S.mw(tok, s_xt[k])
                block.sync(q1)

                ks = [(i * PCT + tc) % NXT for tc in range(PCT)]
                emit_ln_batch(("A", i),
                              [(tc, xt[:, ks[tc] * D:(ks[tc] + 1) * D],
                                [s_xt[ks[tc]]]) for tc in range(PCT)],
                              grp=0)

                def a2(a):
                    for tc in range(PCT):
                        k = ks[tc]
                        S.bw(a, "act", s_ln1b[lb + tc])
                        S.wait(a, "act", TK[(("A", i), "ln", tc)])
                        tok = S.done(a.activation(
                            ln1b[:, (lb + tc) * D:(lb + tc + 1) * D],
                            xt[:, k * D:(k + 1) * D], AF.Identity,
                            bias=nmr[:, tc:tc + 1],
                            scale=rstd[:, tc:tc + 1]), "act")
                        S.mw(tok, s_ln1b[lb + tc])
                        S.mr(tok, s_xt[k], s_stats[tc])
                block.scalar(a2)

            def stage1(i, xjobs=()):
                xjobs = list(xjobs)
                t0 = i * TT
                lb = (i % 2) * PCT

                for half in range(2):
                    for sub in range(2):
                        dcg = half * 2 + sub

                        def pe1(t, dcg=dcg, sub=sub, lb=lb):
                            S.wait(t, "pe", TK["idb"])
                            S.bw(t, "pe", s_psTB[sub])
                            S.br(t, "pe", *s_ln1b[lb:lb + PCT])
                            for j in range(2):
                                dc = dcg * 2 + j
                                for tc in range(PCT):
                                    inst = t.transpose(
                                        psTB[sub][:, (j * PCT + tc) * P:
                                                   (j * PCT + tc + 1) * P],
                                        ln1b[:, (lb + tc) * D + dc * P:
                                             (lb + tc) * D + (dc + 1) * P],
                                        identb[:])
                            tok = S.done(inst, "pe")
                            S.mw(tok, s_psTB[sub])
                            S.mr(tok, *s_ln1b[lb:lb + PCT])
                        block.tensor(pe1)

                    def a3(a, half=half):
                        for sub in range(2):
                            dcg = half * 2 + sub
                            d0 = dcg * 2
                            S.bw(a, "act", s_ln1f[d0], s_ln1f[d0 + 1])
                            S.br(a, "act", s_psTB[sub])
                            tok = S.done(a.copy(
                                ln1f[:, d0 * TT:(d0 + 2) * TT],
                                psTB[sub][:]), "act")
                            S.mw(tok, s_ln1f[d0], s_ln1f[d0 + 1])
                            S.mr(tok, s_psTB[sub])
                    block.scalar(a3)

                pending = []
                for ecp in range(DC // 2):
                    def pe2(t, ecp=ecp):
                        S.wait(t, "pe", TK["wiT"], TK["wgT"])
                        for ec in (2 * ecp, 2 * ecp + 1):
                            ku = (2 * ec) % 4
                            kg = (2 * ec + 1) % 4
                            for bk, wsb in ((ku, wiT), (kg, wgT)):
                                S.bw(t, "pe", s_psA[bk])
                                S.br(t, "pe", *s_ln1f)
                                for dc in range(DC):
                                    inst = t.matmul(
                                        psA[bk][:],
                                        wsb[:, dc * D + ec * P:
                                            dc * D + (ec + 1) * P],
                                        ln1f[:, dc * TT:(dc + 1) * TT],
                                        start=(dc == 0), stop=(dc == DC - 1))
                                tok = S.done(inst, "pe")
                                S.mw(tok, s_psA[bk])
                                for dc in range(DC):
                                    S.mr(tok, s_ln1f[dc])
                    block.tensor(pe2)

                    def a4(a, ecp=ecp):
                        for ec in (2 * ecp, 2 * ecp + 1):
                            ku = (2 * ec) % 4
                            kg = (2 * ec + 1) % 4
                            cu = (ec % NUG) * TT
                            S.bw(a, "act", s_u[ec % NUG])
                            S.br(a, "act", s_psA[ku])
                            tok = S.done(a.activation(
                                uB[:, cu:cu + TT], psA[ku][:], AF.Identity,
                                bias=bi_s[:, ec:ec + 1]), "act")
                            S.mw(tok, s_u[ec % NUG])
                            S.mr(tok, s_psA[ku])
                            S.bw(a, "act", s_g[ec % NUG])
                            S.br(a, "act", s_psA[kg])
                            tokg = S.done(a.activation(
                                gB[:, cu:cu + TT], psA[kg][:], AF.Sigmoid,
                                bias=bg_s[:, ec:ec + 1]), "act")
                            S.mw(tokg, s_g[ec % NUG])
                            S.mr(tokg, s_psA[kg])
                    block.scalar(a4)

                    for ec in pending:
                        emit_scan_chain(i, ec)
                    pending = [2 * ecp, 2 * ecp + 1]
                    for _ in range(2):
                        if xjobs:
                            xjobs.pop(0)()
                for ec in pending:
                    emit_scan_chain(i, ec)
                for job in xjobs:
                    job()

            def emit_scan_chain(i, ec):
                    def g2a(g, ec=ec):
                        cu = (ec % NUG) * TT
                        km = 0
                        S.br(g, "pool", s_u[ec % NUG], s_g[ec % NUG])
                        S.bw(g, "pool", s_gtmp[km])
                        tok = S.done(g.tensor_tensor(
                            gtmp[:, 0:TT],
                            gB[:, cu:cu + TT], uB[:, cu:cu + TT],
                            ALU.mult), "pool")
                        S.mw(tok, s_gtmp[km])
                        S.mr(tok, s_g[ec % NUG], s_u[ec % NUG])
                    block.gpsimd(g2a)

                    def g2(v, ec=ec):
                        cu = (ec % NUG) * TT
                        km = 0
                        S.br(v, "dve", s_gtmp[km], s_u[ec % NUG])
                        S.bw(v, "dve", s_u[ec % NUG])
                        tok = S.done(v.tensor_tensor(
                            uB[:, cu:cu + TT], uB[:, cu:cu + TT],
                            gtmp[:, 0:TT],
                            ALU.subtract), "dve")
                        S.mw(tok, s_u[ec % NUG])
                        S.mr(tok, s_gtmp[km])
                    block.vector(g2)

                    def v3(v, ec=ec, i=i):
                        cu = (ec % NUG) * TT
                        kh = ec % 2
                        S.br(v, "dve", s_u[ec % NUG], s_g[ec % NUG],
                             s_carry[ec])
                        S.bw(v, "dve", s_h32[kh])
                        tok = S.done(v.tensor_tensor_scan(
                            h32[:, kh * TT:(kh + 1) * TT],
                            gB[:, cu:cu + TT], uB[:, cu:cu + TT],
                            carry[:, ec:ec + 1],
                            ALU.mult, ALU.add), "dve")
                        S.mw(tok, s_h32[kh])
                        S.mr(tok, s_u[ec % NUG], s_g[ec % NUG], s_carry[ec])
                        S.bw(v, "dve", s_carry[ec])
                        tok2 = S.done(v.tensor_copy(
                            carry[:, ec:ec + 1],
                            h32[:, (kh + 1) * TT - 1:(kh + 1) * TT]), "dve")
                        S.mw(tok2, s_carry[ec])
                        S.mr(tok2, s_h32[kh])
                    block.vector(v3)

                    def g3(g, ec=ec, i=i):
                        kh = ec % 2
                        pr = i % 2
                        S.br(g, "pool", s_h32[kh])
                        S.bw(g, "pool", s_hbf[pr])
                        tok = S.done(g.tensor_copy(
                            hbf[:, pr * DC * TT + ec * TT:
                                pr * DC * TT + (ec + 1) * TT],
                            h32[:, kh * TT:(kh + 1) * TT]), "pool")
                        if ec == DC - 1:
                            S.mw(tok, s_hbf[pr])
                        TK[("A", i, "hbf", ec)] = tok
                        S.mr(tok, s_h32[kh])
                    block.gpsimd(g3)

            def s2a_fills(i):
                pr = i % 2
                for dch in range(2):
                    def pe3(t, dch=dch):
                        S.wait(t, "pe", TK["woT"])
                        for dc in range(dch * 4, dch * 4 + 4):
                            bk = dc % 4
                            S.bw(t, "pe", s_psA[bk])
                            for ec in range(DC):
                                S.wait(t, "pe", TK[("A", i, "hbf", ec)])
                                inst = t.matmul(
                                    psA[bk][:],
                                    woT[:, ec * D + dc * P: ec * D + (dc + 1) * P],
                                    hbf[:, pr * DC * TT + ec * TT:
                                        pr * DC * TT + (ec + 1) * TT],
                                    start=(ec == 0), stop=(ec == DC - 1))
                            tok = S.done(inst, "pe")
                            S.mw(tok, s_psA[bk])
                            S.mr(tok, s_hbf[pr])
                    block.tensor(pe3)

                    def a5(a, dch=dch):
                        for dc in range(dch * 4, dch * 4 + 4):
                            bk = dc % 4
                            S.bw(a, "act", s_aof[dc])
                            S.br(a, "act", s_psA[bk])
                            tok = S.done(a.activation(
                                aof[:, dc * TT:(dc + 1) * TT], psA[bk][:],
                                AF.Identity, bias=bo_s[:, dc:dc + 1]), "act")
                            S.mw(tok, s_aof[dc])
                            S.mr(tok, s_psA[bk])
                    block.scalar(a5)

            def mk_ao_jobs(i):
                jobs = []
                for tc in range(PCT):
                    for half in range(2):
                        def job(tc=tc, half=half):
                            bk = half

                            def pe4(t):
                                S.wait(t, "pe", TK["idf"])
                                S.bw(t, "pe", s_psTF[bk])
                                dcs = range(half * 4, half * 4 + 4)
                                S.br(t, "pe", *[s_aof[dc] for dc in dcs])
                                for j, dc in enumerate(dcs):
                                    inst = t.transpose(
                                        psTF[bk][:, j * P:(j + 1) * P],
                                        aof[:, dc * TT + tc * P:
                                            dc * TT + tc * P + P],
                                        identf[:])
                                tok = S.done(inst, "pe")
                                S.mw(tok, s_psTF[bk])
                                for dc in dcs:
                                    S.mr(tok, s_aof[dc])
                            block.tensor(pe4)

                            def v4(v):
                                S.br(v, "dve", s_psTF[bk])
                                if half == 0:
                                    S.bw(v, "dve", s_aot[tc])
                                tok = S.done(v.tensor_copy(
                                    aot[:, tc * D + half * 512:
                                        tc * D + (half + 1) * 512],
                                    psTF[bk][:]), "dve")
                                if half == 1:
                                    S.mw(tok, s_aot[tc])
                                S.mr(tok, s_psTF[bk])
                            block.vector(v4)
                        jobs.append(job)
                return jobs

            def s2a_rest(i):
                t0 = i * TT
                emit_ln_batch(("A2", i),
                              [(tc, aot[:, tc * D:(tc + 1) * D], [s_aot[tc]])
                               for tc in range(PCT)], grp=1)

                def a6(a):
                    for tc in range(PCT):
                        S.wait(a, "act", TK[(("A2", i), "ln", tc)])
                        S.bw(a, "act", s_aot[tc])
                        q = 4 + tc
                        tok = S.done(a.activation(
                            aot[:, tc * D:(tc + 1) * D],
                            aot[:, tc * D:(tc + 1) * D], AF.Identity,
                            bias=nmr[:, q:q + 1],
                            scale=rstd[:, q:q + 1]), "act")
                        S.mw(tok, s_aot[tc])
                        S.mr(tok, s_stats[q])
                block.scalar(a6)

                kxs = [(i * PCT + tc) % NXT for tc in range(PCT)]

                def v5(g):
                    for tc in range(PCT):
                        kx = kxs[tc]
                        S.br(g, "pool", s_aot[tc], s_xt[kx])
                        S.bw(g, "pool", s_xt[kx])
                        tok = S.done(g.tensor_tensor(
                            xt[:, kx * D:(kx + 1) * D],
                            aot[:, tc * D:(tc + 1) * D],
                            xt[:, kx * D:(kx + 1) * D], ALU.add), "pool")
                        S.mw(tok, s_xt[kx])
                        S.mr(tok, s_aot[tc])
                block.gpsimd(v5)

                def a7(a):
                    for tc in range(PCT):
                        kx = kxs[tc]
                        S.br(a, "act", s_xt[kx])
                        tok = S.dma_done(a.dma_start(
                            out=x1s[t0 + tc * P: t0 + (tc + 1) * P, :],
                            in_=xt[:, kx * D:(kx + 1) * D]), f"x1w{tc}")
                        S.mr(tok, s_xt[kx])
                block.scalar(a7)

                emit_ln_batch(("A3", i),
                              [(tc, xt[:, kxs[tc] * D:(kxs[tc] + 1) * D],
                                [s_xt[kxs[tc]]])
                               for tc in range(PCT)], grp=2)

                def v6(v):
                    S.wait(v, "dve",
                           *[TK[(("A3", i), "ln", tc)] for tc in range(PCT)])
                    for tc in range(PCT):
                        pcid = i * PCT + tc
                        q = 8 + tc
                        v.tensor_copy(m3_all[:, pcid:pcid + 1],
                                      mv[:, q * 2:q * 2 + 1])
                        tok = S.done(v.tensor_copy(
                            rs3_all[:, pcid:pcid + 1],
                            rstd[:, q:q + 1]), "dve")
                        TK[("A", i, "st3", tc)] = tok
                        S.mr(tok, s_stats[q])
                block.vector(v6)

                def a8(a):
                    for tc in range(PCT):
                        kx = kxs[tc]
                        S.bw(a, "act", s_ln3b[tc])
                        S.br(a, "act", s_xt[kx])
                        S.wait(a, "act", TK[(("A3", i), "ln", tc)])
                        q = 8 + tc
                        tok = S.done(a.activation(
                            ln3b[:, tc * D:(tc + 1) * D],
                            xt[:, kx * D:(kx + 1) * D], AF.Identity,
                            bias=nmr[:, q:q + 1],
                            scale=rstd[:, q:q + 1]), "act")
                        S.mw(tok, s_ln3b[tc])
                        S.mr(tok, s_xt[kx], s_stats[q])
                block.scalar(a8)

            def stage2b(i):
                t0 = i * TT
                for half in range(2):
                    for sub in range(2):
                        dcg = half * 2 + sub

                        def pe5(t, dcg=dcg, sub=sub):
                            S.bw(t, "pe", s_psTB[sub])
                            S.br(t, "pe", *s_ln3b)
                            for j in range(2):
                                dc = dcg * 2 + j
                                for tc in range(PCT):
                                    inst = t.transpose(
                                        psTB[sub][:, (j * PCT + tc) * P:
                                                   (j * PCT + tc + 1) * P],
                                        ln3b[:, tc * D + dc * P:
                                             tc * D + (dc + 1) * P],
                                        identb[:])
                            tok = S.done(inst, "pe")
                            S.mw(tok, s_psTB[sub])
                            S.mr(tok, *s_ln3b)
                        block.tensor(pe5)

                    def a9(a, half=half, i=i):
                        for sub in range(2):
                            dcg = half * 2 + sub
                            S.bw(a, "act", s_ln3f)
                            S.br(a, "act", s_psTB[sub])
                            tok = S.done(a.copy(
                                ln3f[:, sub * 2 * TT:(sub + 1) * 2 * TT],
                                psTB[sub][:]), "act")
                            TK[("A", i, "l3f", half)] = tok
                            S.mr(tok, s_psTB[sub])
                    block.scalar(a9)

                    def q2(sync, i=i, half=half):
                        S.wait(sync, "sp", TK[("A", i, "l3f", half)])
                        s_ln3f.w = TK[("A", i, "l3f", half)]
                        tok = S.dma_done(sync.dma_start(
                            out=ln3s[half * 4:(half + 1) * 4, :, t0:t0 + TT]
                            .rearrange("dc p t -> p dc t"),
                            in_=ln3f[:].rearrange("p (dc t) -> p dc t", t=TT)),
                            "ln3w")
                        S.mr(tok, s_ln3f)
                    block.sync(q2)

                    def v7(v, i=i, half=half):
                        S.wait(v, "dve", TK[("A", i, "l3f", half)])
                        S.bw(v, "dve", s_tmp1[0])
                        S.br(v, "dve", s_rsum)
                        t1 = S.done(v.reduce_sum(
                            tmp1[:, 0:4],
                            ln3f[:].rearrange("p (dc t) -> p dc t", t=TT),
                            axis=AX.X), "dve")
                        S.wait(v, "dve", t1)
                        t2 = S.done(v.tensor_tensor(
                            rsum[:, half * 4:(half + 1) * 4],
                            rsum[:, half * 4:(half + 1) * 4],
                            tmp1[:, 0:4], ALU.add), "dve")
                        S.mw(t2, s_rsum)
                        S.mr(t2, s_ln3f, s_tmp1[0])
                        TK[("A", i, "rsum")] = t2
                    block.vector(v7)

            stage1a(0)
            stage1a(1)
            stage1(0)
            for i in range(NT):
                s2a_fills(i)
                if i + 2 < NT:
                    stage1a(i + 2)
                if i + 1 < NT:
                    stage1(i + 1, mk_ao_jobs(i))
                else:
                    for job in mk_ao_jobs(i):
                        job()
                if i - 1 >= 0:
                    stage2b(i - 1)
                s2a_rest(i)
            stage2b(NT - 1)

        S.barrier(block)

        with ExitStack() as stB:
            def sbb(name, shape, dt=F32):
                return stB.enter_context(nc.sbuf_tensor(name, shape, dt))
            rlog = sbb("rlog", [1, E]); rnmx = sbb("rnmx", [1, 1])
            rexp = sbb("rexp", [1, E]); rz1 = sbb("rz1", [1, 1])
            rw_t = sbb("rw_t", [1, E]); rmx1 = sbb("rmx1", [1, 1])
            rmx2 = sbb("rmx2", [1, 1]); msk1 = sbb("msk1", [1, E])
            msk2 = sbb("msk2", [1, E]); rtmp = sbb("rtmp", [1, E])
            cw_row = sbb("cw_row", [1, E])
            e2t = sbb("e2t", [1, 1]); z2t = sbb("z2t", [1, 1])
            rz2 = sbb("rz2", [1, 1]); w2t = sbb("w2t", [1, 1])
            io8 = stB.enter_context(nc.sbuf_tensor("io8", [1, E], I32))
            io8f = sbb("io8f", [1, E]); seltmp = sbb("seltmp", [1, E])
            idxf = sbb("idxf", [1, 2])
            idxi = stB.enter_context(nc.sbuf_tensor("idxi", [1, 2], I32))
            cw2row = sbb("cw2row", [1, 2]); cwk = sbb("cwk", [P, 2])

            w1_sb = sbb("w1_sb", [P, DC * F], BF16)
            w2_sb = sbb("w2_sb", [P, FC * D], BF16)
            b1t_sb = sbb("b1t_sb", [P, FC]); b2t_sb = sbb("b2t_sb", [P, DC])
            ln3t = sbb("ln3t", [P, DC * TT], BF16)
            hB = sbb("hB", [P, FC * TT], BF16)
            outB = sbb("outB", [P, DC * TT])

            psRscope = ExitStack()
            psR = psRscope.enter_context(nc.psum_tensor("psR", [1, E], F32))
            psBC = psRscope.enter_context(nc.psum_tensor("psBC", [P, E], F32))

            s_w1 = Slot(); s_w2 = Slot(); s_b12 = Slot()
            s_ln3t = Slot()
            s_hB = [Slot() for _ in range(FC)]
            s_outB = [Slot() for _ in range(DC)]
            s_psH = [Slot() for _ in range(4)]
            s_psO = [Slot() for _ in range(3)]

            def g_io(g):
                TK["io8"] = S.done(g.iota(io8[:], [[1, E]],
                                          channel_multiplier=0), "pool")
            block.gpsimd(g_io)

            def v8(v):
                S.wait(v, "dve", TK[("A", NT - 1, "rsum")])
                tok = S.done(v.tensor_scalar(
                    rsum[:], rsum[:], 1.0 / T, None, ALU.mult), "dve")
                TK["rmean"] = tok
            block.vector(v8)

            def pe6(t):
                S.wait(t, "pe", TK["rmean"])
                for dc in range(DC):
                    inst = t.matmul(
                        psR[:], rsum[:, dc:dc + 1],
                        wrt_s[:, dc * E:(dc + 1) * E],
                        start=(dc == 0), stop=(dc == DC - 1))
                TK["rlogm"] = S.done(inst, "pe")
            block.tensor(pe6)

            def v9(v):
                S.wait(v, "dve", TK["rlogm"])
                t = S.done(v.tensor_tensor(rlog[:], psR[:], br_s[:], ALU.add), "dve")
                S.wait(v, "dve", t)
                t = S.done(v.reduce_max(rnmx[:], rlog[:], axis=AX.X,
                                        negate=True), "dve")
                TK["rnmx"] = t
            block.vector(v9)

            def a10(a):
                S.wait(a, "act", TK["rnmx"])
                TK["rexp"] = S.done(a.activation(
                    rexp[:], rlog[:], AF.Exp, bias=rnmx[:]), "act")
            block.scalar(a10)

            def v10(v):
                S.wait(v, "dve", TK["rexp"])
                t = S.done(v.reduce_sum(rz1[:], rexp[:], axis=AX.X), "dve")
                S.wait(v, "dve", t)
                t = S.done(v.reciprocal(rz1[:], rz1[:]), "dve")
                S.wait(v, "dve", t)
                t = S.done(v.tensor_scalar(
                    rw_t[:], rexp[:], rz1[:], None, ALU.mult), "dve")
                S.wait(v, "dve", t)
                t = S.done(v.reduce_max(rmx1[:], rw_t[:], axis=AX.X), "dve")
                S.wait(v, "dve", t)
                t = S.done(v.tensor_scalar(
                    msk1[:], rw_t[:], rmx1[:], None, ALU.is_equal), "dve")
                S.wait(v, "dve", t)
                t = S.done(v.tensor_scalar(
                    rtmp[:], msk1[:], 1e30, None, ALU.mult), "dve")
                S.wait(v, "dve", t)
                t = S.done(v.tensor_tensor(
                    rtmp[:], rw_t[:], rtmp[:], ALU.subtract), "dve")
                S.wait(v, "dve", t)
                t = S.done(v.reduce_max(rmx2[:], rtmp[:], axis=AX.X), "dve")
                S.wait(v, "dve", t)
                t = S.done(v.tensor_scalar(
                    msk2[:], rw_t[:], rmx2[:], None, ALU.is_equal), "dve")
                S.wait(v, "dve", t)
                t = S.done(v.tensor_tensor(
                    e2t[:], rmx2[:], rmx1[:], ALU.subtract), "dve")
                TK["e2t"] = t
            block.vector(v10)

            def a11(a):
                S.wait(a, "act", TK["e2t"])
                TK["e2e"] = S.done(a.activation(e2t[:], e2t[:], AF.Exp), "act")
            block.scalar(a11)

            def v11(v):
                S.wait(v, "dve", TK["e2e"])
                t = S.done(v.tensor_scalar(
                    z2t[:], e2t[:], 1.0, None, ALU.add), "dve")
                S.wait(v, "dve", t)
                t = S.done(v.reciprocal(rz2[:], z2t[:]), "dve")
                S.wait(v, "dve", t)
                t = S.done(v.tensor_tensor(w2t[:], e2t[:], rz2[:], ALU.mult), "dve")
                S.wait(v, "dve", t)
                S.wait(v, "dve", TK["io8"])
                t = S.done(v.tensor_copy(io8f[:], io8[:]), "dve")
                S.wait(v, "dve", t)
                t = S.done(v.tensor_tensor(
                    seltmp[:], msk1[:], io8f[:], ALU.mult), "dve")
                S.wait(v, "dve", t)
                t = S.done(v.reduce_sum(idxf[:, 0:1], seltmp[:], axis=AX.X), "dve")
                S.wait(v, "dve", t)
                t = S.done(v.tensor_tensor(
                    seltmp[:], msk2[:], io8f[:], ALU.mult), "dve")
                S.wait(v, "dve", t)
                t = S.done(v.reduce_sum(idxf[:, 1:2], seltmp[:], axis=AX.X), "dve")
                S.wait(v, "dve", t)
                TK["idxi"] = S.done(v.tensor_copy(idxi[:], idxf[:]), "dve")
                v.tensor_copy(cw2row[:, 0:1], rz2[:])
                TK["cw2"] = S.done(v.tensor_copy(cw2row[:, 1:2], w2t[:]), "dve")
                t = S.done(v.tensor_scalar(
                    msk1[:], msk1[:], rz2[:], None, ALU.mult), "dve")
                t = S.done(v.tensor_scalar(
                    msk2[:], msk2[:], w2t[:], None, ALU.mult), "dve")
                S.wait(v, "dve", t)
                t = S.done(v.tensor_tensor(
                    cw_row[:], msk1[:], msk2[:], ALU.add), "dve")
                TK["cw"] = t
            block.vector(v11)

            def pe7(t):
                S.wait(t, "pe", TK["cw"], TK["cw2"])
                TK["cwm"] = S.done(t.matmul(
                    psBC[:], ones1f[:], cw_row[:], start=True, stop=True), "pe")
            block.tensor(pe7)

            def v12(v):
                S.wait(v, "dve", TK["cwm"])
                TK["cwb"] = S.done(v.tensor_copy(cwb[:], psBC[:]), "dve")
            block.vector(v12)

            def pe7b(t):
                S.wait(t, "pe", TK["cwb"])
                TK["cwm2"] = S.done(t.matmul(
                    psBC[:, 0:2], ones1f[:], cw2row[:], start=True, stop=True),
                    "pe")
            block.tensor(pe7b)

            def v12b(v):
                S.wait(v, "dve", TK["cwm2"])
                TK["cwk"] = S.done(v.tensor_copy(cwk[:], psBC[:, 0:2]), "dve")
            block.vector(v12b)

            psRscope.close()
            psH = [stB.enter_context(nc.psum_tensor(f"psH{i}", [P, TT], F32))
                   for i in range(4)]
            psO = [stB.enter_context(nc.psum_tensor(f"psO{i}", [P, TT], F32))
                   for i in range(3)]

            NEXP = 2 if SPARSE else E
            REGS = {}
            for e in range(NEXP):
                if SPARSE:
                    def gw(g, e=e):
                        if "r_idx" not in REGS:
                            REGS["r_idx"] = g.alloc_register("r_idx")
                            REGS["r_off"] = g.alloc_register("r_off")
                        r_idx, r_off = REGS["r_idx"], REGS["r_off"]
                        S.wait(g, "pool", TK["idxi"])
                        g.reg_load(r_idx, idxi[0:1, e:e + 1])
                        S.bw(g, "pool", s_w1)
                        for j in range(2):
                            g.reg_mul(r_off, r_idx, D * F)
                            g.reg_add(r_off, r_off, j * 4 * P * F)
                            tok = S.dma_done(g.dma_start(
                                out=w1_sb[:, j * 4 * F:(j + 1) * 4 * F]
                                .rearrange("p (dc f) -> p dc f", f=F),
                                in_=bass.AP(dh["W1"], r_off,
                                            [[F, P], [P * F, 4], [1, F]])),
                                "w1")
                        S.mw(tok, s_w1)
                        TK[("B", e, "w1")] = tok
                        S.bw(g, "pool", s_w2)
                        for j in range(2):
                            g.reg_mul(r_off, r_idx, F * D)
                            g.reg_add(r_off, r_off, j * 16 * P * D)
                            tok = S.dma_done(g.dma_start(
                                out=w2_sb[:, j * 16 * D:(j + 1) * 16 * D]
                                .rearrange("p (fc d) -> p fc d", d=D),
                                in_=bass.AP(dh["W2"], r_off,
                                            [[D, P], [P * D, 16], [1, D]])),
                                "w2")
                        S.mw(tok, s_w2)
                        TK[("B", e, "w2")] = tok
                        S.bw(g, "pool", s_b12)
                        g.reg_mul(r_off, r_idx, P * FC)
                        S.dma_done(g.dma_start(
                            out=b1t_sb[:],
                            in_=bass.AP(dh["b1t"], r_off, [[FC, P], [1, FC]])),
                            "bld")
                        g.reg_mul(r_off, r_idx, P * DC)
                        tok = S.dma_done(g.dma_start(
                            out=b2t_sb[:],
                            in_=bass.AP(dh["b2t"], r_off, [[DC, P], [1, DC]])),
                            "bld")
                        S.mw(tok, s_b12)
                        TK[("B", e, "b12")] = tok
                    block.gpsimd(gw)
                else:
                    def gw(g, e=e):
                        S.bw(g, "pool", s_w1)
                        for dc in range(DC):
                            tok = S.dma_done(g.dma_start(
                                out=w1_sb[:, dc * F:(dc + 1) * F],
                                in_=d["W1"][e, dc * P:(dc + 1) * P, :]), "w1")
                        S.mw(tok, s_w1)
                        TK[("B", e, "w1")] = tok
                        S.bw(g, "pool", s_w2)
                        for fc in range(FC):
                            tok = S.dma_done(g.dma_start(
                                out=w2_sb[:, fc * D:(fc + 1) * D],
                                in_=d["W2"][e, fc * P:(fc + 1) * P, :]), "w2")
                        S.mw(tok, s_w2)
                        TK[("B", e, "w2")] = tok
                    block.gpsimd(gw)

                    def qb(sync, e=e):
                        S.bw(sync, "sp", s_b12)
                        S.dma_done(sync.dma_start(
                            out=b1t_sb[:], in_=d["b1t"][e]), "bld")
                        tok = S.dma_done(sync.dma_start(
                            out=b2t_sb[:], in_=d["b2t"][e]), "bld")
                        S.mw(tok, s_b12)
                        TK[("B", e, "b12")] = tok
                    block.sync(qb)

                for i in range(NT):
                    t0 = i * TT

                    def q3(sync, i=i, e=e):
                        S.bw(sync, "sp", s_ln3t)
                        tok = S.dma_done(sync.dma_start(
                            out=ln3t[:].rearrange("p (dc t) -> p dc t", t=TT),
                            in_=ln3s[:, :, t0:t0 + TT].rearrange(
                                "dc p t -> p dc t")), "ln3r")
                        S.mw(tok, s_ln3t)
                    block.sync(q3)

                    for fcg in range(FC // 4):
                        def pe8(t, fcg=fcg, e=e):
                            S.wait(t, "pe", TK[("B", e, "w1")], TK["cwk"])
                            S.br(t, "pe", s_ln3t)
                            for fc in range(fcg * 4, fcg * 4 + 4):
                                bk = fc % 4
                                S.bw(t, "pe", s_psH[bk])
                                for dc in range(DC):
                                    inst = t.matmul(
                                        psH[bk][:],
                                        w1_sb[:, dc * F + fc * P:
                                              dc * F + (fc + 1) * P],
                                        ln3t[:, dc * TT:(dc + 1) * TT],
                                        start=(dc == 0), stop=(dc == DC - 1))
                                tok = S.done(inst, "pe")
                                S.mw(tok, s_psH[bk])
                                S.mr(tok, s_ln3t, s_w1)
                        block.tensor(pe8)

                        def a12(a, fcg=fcg, e=e):
                            S.wait(a, "act", TK[("B", e, "b12")])
                            for fc in range(fcg * 4, fcg * 4 + 4):
                                bk = fc % 4
                                S.bw(a, "act", s_hB[fc])
                                S.br(a, "act", s_psH[bk])
                                tok = S.done(a.activation(
                                    hB[:, fc * TT:(fc + 1) * TT], psH[bk][:],
                                    AF.Gelu, bias=b1t_sb[:, fc:fc + 1]), "act")
                                S.mw(tok, s_hB[fc])
                                S.mr(tok, s_psH[bk], s_b12)
                        block.scalar(a12)

                    for dcg in ((0, 1, 2), (3, 4, 5), (6, 7)):
                        def pe9(t, dcg=dcg, e=e):
                            S.wait(t, "pe", TK[("B", e, "w2")])
                            S.br(t, "pe", *s_hB)
                            for dc in dcg:
                                bk = dc % 3
                                S.bw(t, "pe", s_psO[bk])
                                for fc in range(FC):
                                    inst = t.matmul(
                                        psO[bk][:],
                                        w2_sb[:, fc * D + dc * P:
                                              fc * D + (dc + 1) * P],
                                        hB[:, fc * TT:(fc + 1) * TT],
                                        start=(fc == 0), stop=(fc == FC - 1))
                                tok = S.done(inst, "pe")
                                S.mw(tok, s_psO[bk])
                                S.mr(tok, s_w2, *s_hB)
                        block.tensor(pe9)

                        def v13(v, dcg=dcg, e=e):
                            S.wait(v, "dve", TK["cwb"], TK["cwk"],
                                   TK[("B", e, "b12")])
                            for dc in dcg:
                                bk = dc % 3
                                S.bw(v, "dve", s_outB[dc])
                                S.br(v, "dve", s_psO[bk])
                                scale_col = (cwk[:, e:e + 1] if SPARSE
                                             else cwb[:, e:e + 1])
                                tok = S.done(v.tensor_scalar(
                                    outB[:, dc * TT:(dc + 1) * TT], psO[bk][:],
                                    b2t_sb[:, dc:dc + 1], scale_col,
                                    ALU.add, ALU.mult), "dve")
                                S.mw(tok, s_outB[dc])
                                S.mr(tok, s_psO[bk], s_b12)
                        block.vector(v13)

                    def q4(g, i=i, e=e):
                        S.br(g, "pool", *s_outB)
                        S.wait(g, "pool", ("q_accw", S.cnt["q_accw"]))
                        tok = S.dma_done(g.dma_start(
                            out=accs[:, :, t0:t0 + TT].rearrange(
                                "dc p t -> p dc t"),
                            in_=outB[:].rearrange("p (dc t) -> p dc t", t=TT),
                            accum_op=(ALU.bypass if e == 0 else ALU.add)),
                            "accw")
                        for sl in s_outB:
                            S.mr(tok, sl)
                        TK[("B", e, i, "acc")] = tok
                    block.gpsimd(q4)

        S.barrier(block)

        with ExitStack() as stC:
            def sbc(name, shape, dt=F32):
                return stC.enter_context(nc.sbuf_tensor(name, shape, dt))
            gam_r = sbc("gam_r", [1, D]); bet_r = sbc("bet_r", [1, D])
            gam_b = sbc("gam_b", [P, D]); bet_b = sbc("bet_b", [P, D])
            accf = sbc("accf", [P, 2 * DC * TT])
            x1c = sbc("x1c", [P, 3 * 4 * D])
            yt = sbc("yt", [P, 2 * 4 * D])
            zt = sbc("zt", [P, 2 * 4 * D])
            ot = sbc("ot", [P, 4 * D])
            tmpc = sbc("tmpc", [P, 4 * D])
            st6c = sbc("st6c", [P, 3 * 48])
            mvc = sbc("mvc", [P, 3 * 8]); sdc = sbc("sdc", [P, 3 * 4])
            rsc = sbc("rsc", [P, 3 * 4])

            psC = [stC.enter_context(nc.psum_tensor(f"psC{i}", [P, 4 * P], F32))
                   for i in range(2)]
            psG = stC.enter_context(nc.psum_tensor("psG", [P, D], F32))

            s_accf = [Slot() for _ in range(2)]
            s_x1c = [Slot() for _ in range(3)]
            s_yt = [Slot() for _ in range(2)]
            s_zt = [Slot() for _ in range(2)]
            s_ot = [Slot()]
            s_ot.append(s_ot[0])
            s_tmpc = [Slot()]
            s_psC = [Slot() for _ in range(2)]
            s_statc = [Slot() for _ in range(12)]

            def qc(sync):
                S.dma_done(sync.dma_start(out=gam_r[:], in_=d["gamma"][:]),
                           "cst")
                tok = S.dma_done(sync.dma_start(out=bet_r[:], in_=d["beta"][:]),
                                 "cst")
                TK["gb"] = tok
            block.sync(qc)

            def pec(t):
                S.wait(t, "pe", TK["gb"])
                t.matmul(psG[:, 0:512], ones1f[:], gam_r[:, 0:512],
                         start=True, stop=True)
                TK["gbm"] = S.done(t.matmul(
                    psG[:, 512:1024], ones1f[:], gam_r[:, 512:1024],
                    start=True, stop=True), "pe")
            block.tensor(pec)

            def vc(v):
                S.wait(v, "dve", TK["gbm"])
                TK["gbe"] = S.done(v.tensor_copy(gam_b[:], psG[:]), "dve")
            block.vector(vc)

            def pec2(t):
                S.wait(t, "pe", TK["gbe"])
                t.matmul(psG[:, 0:512], ones1f[:], bet_r[:, 0:512],
                         start=True, stop=True)
                TK["bbm"] = S.done(t.matmul(
                    psG[:, 512:1024], ones1f[:], bet_r[:, 512:1024],
                    start=True, stop=True), "pe")
            block.tensor(pec2)

            def vc2(v):
                S.wait(v, "dve", TK["bbm"])
                TK["bbe"] = S.done(v.tensor_copy(bet_b[:], psG[:]), "dve")
            block.vector(vc2)

            def emit_ln_c(tag, mk_src, grp):
                def v1(v):
                    for tc in range(PCT):
                        src_ap, slots = mk_src(tc)
                        q = grp * 4 + tc
                        S.bw(v, "dve", s_statc[q])
                        S.br(v, "dve", *slots)
                        v.bn_stats(st6c[:, q * 12:q * 12 + 6],
                                   src_ap[:, 0:512])
                        t = S.done(v.bn_stats(
                            st6c[:, q * 12 + 6:q * 12 + 12],
                            src_ap[:, 512:1024]), "dve")
                        S.wait(v, "dve", t)
                        TK[(tag, "mv", tc)] = S.done(v.bn_aggr(
                            mvc[:, q * 2:q * 2 + 2],
                            st6c[:, q * 12:(q + 1) * 12]), "dve")
                block.vector(v1)

                def a1(a):
                    S.wait(a, "act",
                           *[TK[(tag, "mv", tc)] for tc in range(PCT)])
                    for tc in range(PCT):
                        q = grp * 4 + tc
                        tok = S.done(a.activation(
                            sdc[:, q:q + 1], mvc[:, q * 2 + 1:q * 2 + 2],
                            AF.Sqrt, bias=eps_t[:]), "act")
                    TK[(tag, "sd")] = tok
                block.scalar(a1)

                def v2(v):
                    S.wait(v, "dve", TK[(tag, "sd")])
                    for tc in range(PCT):
                        q = grp * 4 + tc
                        tok = S.done(v.reciprocal(
                            rsc[:, q:q + 1], sdc[:, q:q + 1]), "dve")
                        S.mw(tok, s_statc[q])
                        _, slots = mk_src(tc)
                        for ss in slots:
                            S.mr(tok, ss)
                        TK[(tag, "ln", tc)] = tok
                block.vector(v2)

            def stageC1(i):
                t0 = i * TT
                ki = i % 2

                def q5(sync, i=i, ki=ki, t0=t0):
                    S.bw(sync, "sp", s_accf[ki])
                    tok = S.dma_done(sync.dma_start(
                        out=accf[:, ki * DC * TT:(ki + 1) * DC * TT].rearrange(
                            "p (dc t) -> p dc t", t=TT),
                        in_=accs[:, :, t0:t0 + TT].rearrange("dc p t -> p dc t")),
                        f"acc{ki}")
                    S.mw(tok, s_accf[ki])
                    kx = i % 3
                    S.bw(sync, "sp", s_x1c[kx])
                    tok = S.dma_done(sync.dma_start(
                        out=x1c[:, kx * 4 * D:(kx + 1) * 4 * D].rearrange(
                            "p (tc d) -> p tc d", d=D),
                        in_=x1s.rearrange("(n tc p) d -> n p tc d",
                                          p=P, tc=PCT)[i]), f"x1c{kx}")
                    S.mw(tok, s_x1c[kx])
                block.sync(q5)

                for tc in range(PCT):
                    for half in range(2):
                        bk = half

                        def pe10(t, i=i, ki=ki, tc=tc, half=half, bk=bk):
                            S.wait(t, "pe", TK["idf"])
                            S.bw(t, "pe", s_psC[bk])
                            S.br(t, "pe", s_accf[ki])
                            for j in range(4):
                                dc = half * 4 + j
                                inst = t.transpose(
                                    psC[bk][:, j * P:(j + 1) * P],
                                    accf[:, ki * DC * TT + dc * TT + tc * P:
                                         ki * DC * TT + dc * TT + tc * P + P],
                                    identf[:])
                            tok = S.done(inst, "pe")
                            S.mw(tok, s_psC[bk])
                            S.mr(tok, s_accf[ki])
                        block.tensor(pe10)

                        def ac0(a, ki=ki, tc=tc, half=half, bk=bk):
                            S.br(a, "act", s_psC[bk])
                            if half == 0 and tc == 0:
                                S.bw(a, "act", s_yt[ki])
                            tok = S.done(a.copy(
                                yt[:, ki * 4 * D + tc * D + half * 512:
                                   ki * 4 * D + tc * D + (half + 1) * 512],
                                psC[bk][:]), "act")
                            if half == 1 and tc == PCT - 1:
                                S.mw(tok, s_yt[ki])
                            S.mr(tok, s_psC[bk])
                        block.scalar(ac0)

            def stageC2a(i):
                ki = i % 2
                kx = i % 3
                def vc4(v, i=i, ki=ki, kx=kx):
                    S.bw(v, "dve", s_tmpc[0])
                    S.br(v, "dve", s_x1c[kx])
                    for tc in range(PCT):
                        pcid = i * PCT + tc
                        S.wait(v, "dve", TK[("A", i, "st3", tc)])
                        tok = S.done(v.tensor_scalar(
                            tmpc[:, tc * D:(tc + 1) * D],
                            x1c[:, kx * 4 * D + tc * D:
                                kx * 4 * D + (tc + 1) * D],
                            m3_all[:, pcid:pcid + 1],
                            rs3_all[:, pcid:pcid + 1],
                            ALU.subtract, ALU.mult), "dve")
                    S.mw(tok, s_tmpc[0])
                    S.mr(tok, s_x1c[kx])
                block.vector(vc4)

                def gc1(v, ki=ki):
                    S.br(v, "dve", s_tmpc[0], s_yt[ki])
                    S.bw(v, "dve", s_yt[ki])
                    for tc in range(PCT):
                        tok = S.done(v.tensor_tensor(
                            yt[:, ki * 4 * D + tc * D:
                               ki * 4 * D + (tc + 1) * D],
                            yt[:, ki * 4 * D + tc * D:
                               ki * 4 * D + (tc + 1) * D],
                            tmpc[:, tc * D:(tc + 1) * D], ALU.add), "dve")
                    S.mw(tok, s_yt[ki])
                    S.mr(tok, s_tmpc[0])
                block.vector(gc1)

                emit_ln_c(("Cy", i),
                          lambda tc, ki=ki: (
                              yt[:, ki * 4 * D + tc * D:
                                 ki * 4 * D + (tc + 1) * D], [s_yt[ki]]),
                          grp=ki)

            def stageC2b(i):
                ki = i % 2

                def vc5(v, i=i, ki=ki):
                    S.bw(v, "dve", s_zt[ki])
                    S.br(v, "dve", s_yt[ki])
                    for tc in range(PCT):
                        q = ki * 4 + tc
                        S.wait(v, "dve", TK[(("Cy", i), "ln", tc)])
                        tok = S.done(v.tensor_scalar(
                            zt[:, ki * 4 * D + tc * D:
                               ki * 4 * D + (tc + 1) * D],
                            yt[:, ki * 4 * D + tc * D:
                               ki * 4 * D + (tc + 1) * D],
                            mvc[:, q * 2:q * 2 + 1],
                            rsc[:, q:q + 1],
                            ALU.subtract, ALU.mult), "dve")
                        S.mr(tok, s_statc[q])
                    S.mw(tok, s_zt[ki])
                    S.mr(tok, s_yt[ki])
                block.vector(vc5)

                def gc2(v, ki=ki):
                    S.wait(v, "dve", TK["gbe"], TK["bbe"])
                    S.br(v, "dve", s_zt[ki])
                    S.bw(v, "dve", s_zt[ki])
                    for tc in range(PCT):
                        t1 = S.done(v.tensor_tensor(
                            zt[:, ki * 4 * D + tc * D:
                               ki * 4 * D + (tc + 1) * D],
                            zt[:, ki * 4 * D + tc * D:
                               ki * 4 * D + (tc + 1) * D],
                            gam_b[:], ALU.mult), "dve")
                        S.wait(v, "dve", t1)
                        tok = S.done(v.tensor_tensor(
                            zt[:, ki * 4 * D + tc * D:
                               ki * 4 * D + (tc + 1) * D],
                            zt[:, ki * 4 * D + tc * D:
                               ki * 4 * D + (tc + 1) * D],
                            bet_b[:], ALU.add), "dve")
                    S.mw(tok, s_zt[ki])
                block.vector(gc2)

                emit_ln_c(("Cz", i),
                          lambda tc, ki=ki: (
                              zt[:, ki * 4 * D + tc * D:
                                 ki * 4 * D + (tc + 1) * D], [s_zt[ki]]),
                          grp=2)

            def stageC2c(i):
                t0 = i * TT
                ki = i % 2
                kx = i % 3
                def vc7(v, i=i, ki=ki):
                    S.bw(v, "dve", s_ot[ki])
                    S.br(v, "dve", s_zt[ki])
                    for tc in range(PCT):
                        q = 2 * 4 + tc
                        S.wait(v, "dve", TK[(("Cz", i), "ln", tc)])
                        tok = S.done(v.tensor_scalar(
                            ot[:, tc * D:(tc + 1) * D],
                            zt[:, ki * 4 * D + tc * D:
                               ki * 4 * D + (tc + 1) * D],
                            mvc[:, q * 2:q * 2 + 1],
                            rsc[:, q:q + 1],
                            ALU.subtract, ALU.mult), "dve")
                        S.mr(tok, s_statc[q])
                    S.mw(tok, s_ot[ki])
                    S.mr(tok, s_zt[ki])
                block.vector(vc7)

                def gc3(g, ki=ki, kx=kx):
                    S.br(g, "pool", s_ot[ki], s_x1c[kx])
                    S.bw(g, "pool", s_ot[ki])
                    for tc in range(PCT):
                        tok = S.done(g.tensor_tensor(
                            ot[:, tc * D:(tc + 1) * D],
                            ot[:, tc * D:(tc + 1) * D],
                            x1c[:, kx * 4 * D + tc * D:
                                kx * 4 * D + (tc + 1) * D], ALU.add), "pool")
                    S.mw(tok, s_ot[ki])
                    S.mr(tok, s_x1c[kx])
                block.gpsimd(gc3)

                def ac3(a, i=i, ki=ki, t0=t0):
                    S.br(a, "act", s_ot[ki])
                    for tc in range(PCT):
                        tok = S.dma_done(a.dma_start(
                            out=out_d[t0 + tc * P: t0 + (tc + 1) * P, :],
                            in_=ot[:, tc * D:(tc + 1) * D]),
                            f"outw{ki}")
                        S.mr(tok, s_ot[ki])
                block.scalar(ac3)

            for k in range(NT + 3):
                if 3 <= k:
                    stageC2c(k - 3)
                if 2 <= k < NT + 2:
                    stageC2b(k - 2)
                if 1 <= k < NT + 1:
                    stageC2a(k - 1)
                if k < NT:
                    stageC1(k)

            def qf(sync):
                S.wait(sync, "sp", ("q_outw0", S.cnt["q_outw0"]),
                       ("q_outw1", S.cnt["q_outw1"]))
            block.sync(qf)

    nc.finalize()
    return nc


def kernel(L0_attn_x, L0_attn_Wi, L0_attn_bi, L0_attn_Wg, L0_attn_bg,
           L0_attn_Wo, L0_attn_bo, L0_ffn_W1, L0_ffn_b1, L0_ffn_W2,
           L0_ffn_b2, L0_ffn_Wr, L0_ffn_br, L0_ffn_gamma, L0_ffn_beta):
    x = np.asarray(L0_attn_x, dtype=np.float32)
    B, T, _ = x.shape
    fm = lambda v: np.ascontiguousarray(
        np.asarray(v, dtype=np.float32).reshape(DC, P).T)
    common = {
        "WiT": np.ascontiguousarray(np.asarray(L0_attn_Wi, np.float32).T),
        "WgT": np.ascontiguousarray(np.asarray(L0_attn_Wg, np.float32).T),
        "WoT": np.ascontiguousarray(np.asarray(L0_attn_Wo, np.float32).T),
        "bi": fm(L0_attn_bi), "bg": fm(L0_attn_bg), "bo": fm(L0_attn_bo),
        "W1": np.ascontiguousarray(np.asarray(L0_ffn_W1, np.float32)),
        "b1t": np.ascontiguousarray(
            np.asarray(L0_ffn_b1, np.float32).reshape(E, FC, P)
            .transpose(0, 2, 1)),
        "W2": np.ascontiguousarray(np.asarray(L0_ffn_W2, np.float32)),
        "b2t": np.ascontiguousarray(
            np.asarray(L0_ffn_b2, np.float32).reshape(E, DC, P)
            .transpose(0, 2, 1)),
        "Wrt": np.ascontiguousarray(
            np.asarray(L0_ffn_Wr, np.float32).reshape(DC, P, E)
            .transpose(1, 0, 2)),
        "br": np.asarray(L0_ffn_br, np.float32).reshape(1, E),
        "gamma": np.asarray(L0_ffn_gamma, np.float32).reshape(1, D),
        "beta": np.asarray(L0_ffn_beta, np.float32).reshape(1, D),
    }
    nc = build(T)
    in_maps = [dict(common, x=np.ascontiguousarray(x[b])) for b in range(B)]
    res = run_bass_kernel_spmd(nc, in_maps, core_ids=list(range(B)))
    return np.stack([np.asarray(r["out"], dtype=np.float32)
                     for r in res.results], axis=0)


# version 18
# speedup vs baseline: 1.0361x; 1.0361x over previous
import numpy as np
from contextlib import ExitStack

import concourse.bass as bass
import concourse.mybir as mybir
from concourse.bass_utils import run_bass_kernel_spmd

F32 = mybir.dt.float32
BF16 = mybir.dt.bfloat16
I32 = mybir.dt.int32
AF = mybir.ActivationFunctionType
ALU = mybir.AluOpType
AX = mybir.AxisListType

SPARSE = True

P = 128
D = 1024
DC = D // P
F = 4096
FC = F // P
E = 8
TT = 512
PCT = TT // P
EPS = 1e-5


class Slot:
    __slots__ = ("w", "r")

    def __init__(self):
        self.w = None
        self.r = []


class Sync:

    def __init__(self, nc, stack):
        self.nc = nc
        self.stack = stack
        self.sems = {}
        self.cnt = {}
        for name in ("pe", "act", "dve", "pool", "sp"):
            self._mk(name)
        self.last_wait = {}

    def _mk(self, name):
        self.sems[name] = self.stack.enter_context(self.nc.semaphore("S_" + name))
        self.cnt[name] = 0

    def stream(self, name):
        key = "q_" + name
        if key not in self.sems:
            self._mk(key)
        return key

    def done(self, inst, eng):
        self.cnt[eng] += 1
        inst.then_inc(self.sems[eng], 1)
        return (eng, self.cnt[eng])

    def dma_done(self, inst, stream):
        key = "q_" + stream
        self.cnt[key] += 16
        inst.then_inc(self.sems[key], 16)
        return (key, self.cnt[key])

    def wait(self, eng_obj, waiter, *toks):
        need = {}
        for t in toks:
            if t is None:
                continue
            s, v = t
            if v > need.get(s, -1):
                need[s] = v
        for s, v in sorted(need.items()):
            key = (waiter, s)
            if self.last_wait.get(key, -1) >= v:
                continue
            eng_obj.wait_ge(self.sems[s], v)
            self.last_wait[key] = v

    def bw(self, eng_obj, waiter, *slots):
        toks = []
        for s in slots:
            if s.w is not None:
                toks.append(s.w)
            toks.extend(s.r)
        self.wait(eng_obj, waiter, *toks)

    def br(self, eng_obj, waiter, *slots):
        self.wait(eng_obj, waiter, *[s.w for s in slots if s.w is not None])

    @staticmethod
    def mw(tok, *slots):
        for s in slots:
            s.w = tok
            s.r = []

    @staticmethod
    def mr(tok, *slots):
        for s in slots:
            s.r.append(tok)

    def barrier(self, block, streams=()):
        toks = [(k, v) for k, v in self.cnt.items() if v > 0]
        for name, opener in (("pe", block.tensor), ("act", block.scalar),
                             ("dve", block.vector), ("pool", block.gpsimd),
                             ("sp", block.sync)):
            def f(eng, name=name):
                self.wait(eng, name, *toks)
            opener(f)


def build(T=8192):
    NT = T // TT
    NPC = T // P
    nc = bass.Bass("TRN2", target_bir_lowering=False)

    dt_in = {
        "x": ([T, D], F32), "WiT": ([D, D], F32), "WgT": ([D, D], F32),
        "WoT": ([D, D], F32), "bi": ([P, DC], F32), "bg": ([P, DC], F32),
        "bo": ([P, DC], F32), "W1": ([E, D, F], F32), "b1t": ([E, P, FC], F32),
        "W2": ([E, F, D], F32), "b2t": ([E, P, DC], F32),
        "Wrt": ([P, DC, E], F32), "br": ([1, E], F32),
        "gamma": ([1, D], F32), "beta": ([1, D], F32),
    }
    dh = {k: nc.dram_tensor(k, sh, dt, kind="ExternalInput")
          for k, (sh, dt) in dt_in.items()}
    d = {k: h.ap() for k, h in dh.items()}
    x1s = nc.dram_tensor("x1s", [T, D], F32, kind="Internal").ap()
    ln3s = nc.dram_tensor("ln3s", [DC, P, T], BF16, kind="Internal").ap()
    accs = nc.dram_tensor("accs", [DC, P, T], F32, kind="Internal").ap()
    out_d = nc.dram_tensor("out", [T, D], F32, kind="ExternalOutput").ap()

    st = ExitStack()
    with st:
        S = Sync(nc, st)
        for q in ("cst", "x1w0", "x1w1", "x1w2", "x1w3", "ln3w", "ln3r",
                  "w1", "w2", "bld", "accw", "acc0", "acc1", "x1c0", "x1c1",
                  "x1c2", "outw0", "outw1"):
            S.stream(q)
        for i in range(8):
            S.stream(f"xt{i}")
        block = st.enter_context(nc.Block())

        def sb(name, shape, dt=F32):
            return st.enter_context(nc.sbuf_tensor(name, shape, dt))

        identb = sb("identb", [P, P], BF16)
        identf = sb("identf", [P, P], F32)
        eps_t = sb("eps_t", [P, 1], F32)
        ones1f = sb("ones1f", [1, P], F32)
        bi_s = sb("bi_s", [P, DC]); bg_s = sb("bg_s", [P, DC])
        bgn_s = sb("bgn_s", [P, DC]); bo_s = sb("bo_s", [P, DC])
        wrt_s = sb("wrt_s", [P, DC * E]); br_s = sb("br_s", [1, E])
        carry = sb("carry", [P, DC])
        rsum = sb("rsum", [P, DC])
        m3_all = sb("m3_all", [P, NPC]); rs3_all = sb("rs3_all", [P, NPC])
        cwb = sb("cwb", [P, E])

        TK = {}

        def sync_startup(sync):
            TK["bi"] = S.dma_done(sync.dma_start(out=bi_s[:], in_=d["bi"][:]), "cst")
            S.dma_done(sync.dma_start(out=bg_s[:], in_=d["bg"][:]), "cst")
            S.dma_done(sync.dma_start(out=bo_s[:], in_=d["bo"][:]), "cst")
            S.dma_done(sync.dma_start(
                out=wrt_s[:].rearrange("p (dc e) -> p dc e", e=E),
                in_=d["Wrt"][:]), "cst")
            TK["cst"] = S.dma_done(sync.dma_start(out=br_s[:], in_=d["br"][:]), "cst")
        block.sync(sync_startup)

        with nc.sbuf_tensor("io_r", [P, P], I32) as io_r, \
             nc.sbuf_tensor("io_c", [P, 1], I32) as io_c, \
             nc.sbuf_tensor("io_rf", [P, P], F32) as io_rf, \
             nc.sbuf_tensor("io_cf", [P, 1], F32) as io_cf:

            def g_start(g):
                TK["ior"] = S.done(g.iota(io_r[:], [[1, P]], channel_multiplier=0), "pool")
                TK["ioc"] = S.done(g.iota(io_c[:], [[0, 1]], channel_multiplier=1), "pool")
            block.gpsimd(g_start)

            def v_start(v):
                S.wait(v, "dve", TK["ior"], TK["ioc"])
                t1 = S.done(v.tensor_copy(io_rf[:], io_r[:]), "dve")
                t2 = S.done(v.tensor_copy(io_cf[:], io_c[:]), "dve")
                S.wait(v, "dve", t1, t2)
                t3 = S.done(v.tensor_scalar(
                    identf[:], io_rf[:], io_cf[:], None, ALU.is_equal), "dve")
                S.wait(v, "dve", t3)
                TK["idb"] = S.done(v.tensor_copy(identb[:], identf[:]), "dve")
                TK["idf"] = t3
                S.done(v.memset(eps_t[:], EPS), "dve")
                S.done(v.memset(ones1f[:], 1.0), "dve")
                S.done(v.memset(carry[:], 0.0), "dve")
                TK["z"] = S.done(v.memset(rsum[:], 0.0), "dve")
                S.wait(v, "dve", TK["cst"], TK["bi"])
                TK["bgn"] = S.done(v.tensor_scalar(
                    bgn_s[:], bg_s[:], -1.0, None, ALU.mult), "dve")
            block.vector(v_start)

        S.barrier(block)

        with ExitStack() as stA:
            def sba(name, shape, dt=F32):
                return stA.enter_context(nc.sbuf_tensor(name, shape, dt))
            NXT = 12
            xt = sba("xt", [P, NXT * D])
            wiT = sba("wiT", [P, D * DC], BF16)
            wgT = sba("wgT", [P, D * DC], BF16)
            woT = sba("woT", [P, D * DC], BF16)
            ln1b = sba("ln1b", [P, 8 * D], BF16)
            ln1f = sba("ln1f", [P, DC * TT], BF16)
            NUG = 4
            uB = sba("uB", [P, NUG * TT]); gB = sba("gB", [P, NUG * TT])
            gtmp = sba("gtmp", [P, TT])
            h32 = sba("h32", [P, 2 * TT])
            hbf = sba("hbf", [P, 2 * DC * TT], BF16)
            aof = sba("aof", [P, DC * TT])
            aot = sba("aot", [P, 4 * D])
            ln3b = sba("ln3b", [P, 4 * D], BF16)
            ln3f = sba("ln3f", [P, 4 * TT], BF16)
            st6 = sba("st6", [P, 12 * 12])
            mv = sba("mv", [P, 12 * 2])
            sdv = sba("sdv", [P, 12]); rstd = sba("rstd", [P, 12])
            tmp1 = sba("tmp1", [P, DC])

            psA = [stA.enter_context(nc.psum_tensor(f"psA{i}", [P, TT], F32))
                   for i in range(4)]
            psTB = [stA.enter_context(nc.psum_tensor(f"psTB{i}", [P, 8 * P], BF16))
                    for i in range(2)]
            psTF = [stA.enter_context(nc.psum_tensor(f"psTF{i}", [P, 4 * P], F32))
                    for i in range(2)]

            s_xt = [Slot() for _ in range(NXT)]
            s_ln1b = [Slot() for _ in range(8)]
            s_ln1f = [Slot() for _ in range(DC)]
            s_u = [Slot() for _ in range(NUG)]
            s_g = [Slot() for _ in range(NUG)]
            s_gtmp = [Slot()]
            s_h32 = [Slot() for _ in range(2)]
            s_hbf = [Slot() for _ in range(2)]
            s_aof = [Slot() for _ in range(DC)]
            s_aot = [Slot() for _ in range(4)]
            s_ln3b = [Slot() for _ in range(4)]
            s_ln3f = Slot()
            s_psA = [Slot() for _ in range(4)]
            s_psTB = [Slot() for _ in range(2)]
            s_psTF = [Slot() for _ in range(2)]
            s_carry = [Slot() for _ in range(DC)]
            s_stats = [Slot() for _ in range(12)]
            s_rsum = Slot()
            s_tmp1 = [Slot() for _ in range(2)]

            def g_wload(g):
                S.dma_done(g.dma_start(
                    out=wiT[:].rearrange("p (dc e) -> p dc e", e=D),
                    in_=d["WiT"].rearrange("(dc p) e -> p dc e", p=P)), "w1")
                S.dma_done(g.dma_start(
                    out=wgT[:].rearrange("p (dc e) -> p dc e", e=D),
                    in_=d["WgT"].rearrange("(dc p) e -> p dc e", p=P)), "w1")
                tok = S.dma_done(g.dma_start(
                    out=woT[:].rearrange("p (dc e) -> p dc e", e=D),
                    in_=d["WoT"].rearrange("(dc p) e -> p dc e", p=P)), "w1")
                TK["wiT"] = TK["wgT"] = TK["woT"] = tok
            block.gpsimd(g_wload)

            def ln_chain(v_or_none, tc, src_ap, want_bias=True):
                pass

            def emit_ln_stats(tile_tag, tc, src_ap, src_slots):
                sl = s_stats[tc]

                def v1(v):
                    S.bw(v, "dve", sl)
                    S.br(v, "dve", *src_slots)
                    v.bn_stats(st6[:, tc * 12:tc * 12 + 6], src_ap[:, 0:512])
                    t = S.done(v.bn_stats(
                        st6[:, tc * 12 + 6:tc * 12 + 12], src_ap[:, 512:1024]), "dve")
                    S.wait(v, "dve", t)
                    t2 = S.done(v.bn_aggr(
                        mv[:, tc * 2:tc * 2 + 2], st6[:, tc * 12:(tc + 1) * 12]), "dve")
                    TK[(tile_tag, "mv", tc)] = t2
                block.vector(v1)

                def a1(a):
                    S.wait(a, "act", TK[(tile_tag, "mv", tc)])
                    TK[(tile_tag, "sd", tc)] = S.done(a.activation(
                        sdv[:, tc:tc + 1], mv[:, tc * 2 + 1:tc * 2 + 2],
                        AF.Sqrt, bias=eps_t[:]), "act")
                block.scalar(a1)

                def v2(v):
                    S.wait(v, "dve", TK[(tile_tag, "sd", tc)])
                    t2 = S.done(v.reciprocal(rstd[:, tc:tc + 1],
                                             sdv[:, tc:tc + 1]), "dve")
                    S.mw(t2, sl)
                    for ss in src_slots:
                        S.mr(t2, ss)
                    TK[(tile_tag, "ln", tc)] = t2
                block.vector(v2)

            def emit_ln_batch(tag, specs, grp):
                def v1(v):
                    for tc, src_ap, slots in specs:
                        q = grp * 4 + tc
                        S.bw(v, "dve", s_stats[q])
                        S.br(v, "dve", *slots)
                        v.bn_stats(st6[:, q * 12:q * 12 + 6], src_ap[:, 0:512])
                        t = S.done(v.bn_stats(
                            st6[:, q * 12 + 6:q * 12 + 12],
                            src_ap[:, 512:1024]), "dve")
                        S.wait(v, "dve", t)
                        TK[(tag, "mv", tc)] = S.done(v.bn_aggr(
                            mv[:, q * 2:q * 2 + 2],
                            st6[:, q * 12:(q + 1) * 12]), "dve")
                block.vector(v1)

                def a1(a):
                    S.wait(a, "act",
                           *[TK[(tag, "mv", tc)] for tc, _, _ in specs])
                    for tc, _, _ in specs:
                        q = grp * 4 + tc
                        tok = S.done(a.activation(
                            sdv[:, q:q + 1], mv[:, q * 2 + 1:q * 2 + 2],
                            AF.Sqrt, bias=eps_t[:]), "act")
                    TK[(tag, "sd")] = tok
                block.scalar(a1)

                def v2(v):
                    S.wait(v, "dve", TK[(tag, "sd")])
                    for tc, _, slots in specs:
                        q = grp * 4 + tc
                        tok = S.done(v.reciprocal(
                            rstd[:, q:q + 1], sdv[:, q:q + 1]), "dve")
                        S.mw(tok, s_stats[q])
                        for ss in slots:
                            S.mr(tok, ss)
                        TK[(tag, "ln", tc)] = tok
                block.vector(v2)

            def stage1a(i):
                t0 = i * TT
                lb = (i % 2) * PCT

                def q1(sync):
                    for tc in range(PCT):
                        k = (i * PCT + tc) % NXT
                        S.bw(sync, "sp", s_xt[k])
                        tok = S.dma_done(sync.dma_start(
                            out=xt[:, k * D:(k + 1) * D],
                            in_=d["x"][t0 + tc * P: t0 + (tc + 1) * P, :]),
                            f"xt{k % 8}")
                        S.mw(tok, s_xt[k])
                block.sync(q1)

                ks = [(i * PCT + tc) % NXT for tc in range(PCT)]
                emit_ln_batch(("A", i),
                              [(tc, xt[:, ks[tc] * D:(ks[tc] + 1) * D],
                                [s_xt[ks[tc]]]) for tc in range(PCT)],
                              grp=0)

                def a2(v):
                    for tc in range(PCT):
                        k = ks[tc]
                        S.bw(v, "dve", s_ln1b[lb + tc])
                        S.wait(v, "dve", TK[(("A", i), "ln", tc)])
                        tok = S.done(v.tensor_scalar(
                            ln1b[:, (lb + tc) * D:(lb + tc + 1) * D],
                            xt[:, k * D:(k + 1) * D],
                            mv[:, tc * 2:tc * 2 + 1],
                            rstd[:, tc:tc + 1],
                            ALU.subtract, ALU.mult), "dve")
                        S.mw(tok, s_ln1b[lb + tc])
                        S.mr(tok, s_xt[k], s_stats[tc])
                block.vector(a2)

            def stage1(i, xjobs=()):
                xjobs = list(xjobs)
                t0 = i * TT
                lb = (i % 2) * PCT

                for half in range(2):
                    for sub in range(2):
                        dcg = half * 2 + sub

                        def pe1(t, dcg=dcg, sub=sub, lb=lb):
                            S.wait(t, "pe", TK["idb"])
                            S.bw(t, "pe", s_psTB[sub])
                            S.br(t, "pe", *s_ln1b[lb:lb + PCT])
                            for j in range(2):
                                dc = dcg * 2 + j
                                for tc in range(PCT):
                                    inst = t.transpose(
                                        psTB[sub][:, (j * PCT + tc) * P:
                                                   (j * PCT + tc + 1) * P],
                                        ln1b[:, (lb + tc) * D + dc * P:
                                             (lb + tc) * D + (dc + 1) * P],
                                        identb[:])
                            tok = S.done(inst, "pe")
                            S.mw(tok, s_psTB[sub])
                            S.mr(tok, *s_ln1b[lb:lb + PCT])
                        block.tensor(pe1)

                    def a3(a, half=half):
                        for sub in range(2):
                            dcg = half * 2 + sub
                            d0 = dcg * 2
                            S.bw(a, "act", s_ln1f[d0], s_ln1f[d0 + 1])
                            S.br(a, "act", s_psTB[sub])
                            tok = S.done(a.copy(
                                ln1f[:, d0 * TT:(d0 + 2) * TT],
                                psTB[sub][:]), "act")
                            S.mw(tok, s_ln1f[d0], s_ln1f[d0 + 1])
                            S.mr(tok, s_psTB[sub])
                    block.scalar(a3)

                pending = []
                for ecp in range(DC // 2):
                    def pe2(t, ecp=ecp):
                        S.wait(t, "pe", TK["wiT"], TK["wgT"])
                        for ec in (2 * ecp, 2 * ecp + 1):
                            ku = (2 * ec) % 4
                            kg = (2 * ec + 1) % 4
                            for bk, wsb in ((ku, wiT), (kg, wgT)):
                                S.bw(t, "pe", s_psA[bk])
                                for dc in range(DC):
                                    S.br(t, "pe", s_ln1f[dc])
                                    inst = t.matmul(
                                        psA[bk][:],
                                        wsb[:, dc * D + ec * P:
                                            dc * D + (ec + 1) * P],
                                        ln1f[:, dc * TT:(dc + 1) * TT],
                                        start=(dc == 0), stop=(dc == DC - 1))
                                tok = S.done(inst, "pe")
                                S.mw(tok, s_psA[bk])
                                for dc in range(DC):
                                    S.mr(tok, s_ln1f[dc])
                    block.tensor(pe2)

                    def a4(a, ecp=ecp):
                        for ec in (2 * ecp, 2 * ecp + 1):
                            ku = (2 * ec) % 4
                            kg = (2 * ec + 1) % 4
                            cu = (ec % NUG) * TT
                            S.bw(a, "act", s_u[ec % NUG])
                            S.br(a, "act", s_psA[ku])
                            tok = S.done(a.activation(
                                uB[:, cu:cu + TT], psA[ku][:], AF.Identity,
                                bias=bi_s[:, ec:ec + 1]), "act")
                            S.mw(tok, s_u[ec % NUG])
                            S.mr(tok, s_psA[ku])
                            S.bw(a, "act", s_g[ec % NUG])
                            S.br(a, "act", s_psA[kg])
                            tokg = S.done(a.activation(
                                gB[:, cu:cu + TT], psA[kg][:], AF.Sigmoid,
                                bias=bg_s[:, ec:ec + 1]), "act")
                            S.mw(tokg, s_g[ec % NUG])
                            S.mr(tokg, s_psA[kg])
                    block.scalar(a4)

                    for ec in pending:
                        emit_scan_chain(i, ec)
                    pending = [2 * ecp, 2 * ecp + 1]
                    for _ in range(2):
                        if xjobs:
                            xjobs.pop(0)()
                for ec in pending:
                    emit_scan_chain(i, ec)
                for job in xjobs:
                    job()

            def emit_scan_chain(i, ec):
                    def g2a(g, ec=ec):
                        cu = (ec % NUG) * TT
                        km = 0
                        S.br(g, "pool", s_u[ec % NUG], s_g[ec % NUG])
                        S.bw(g, "pool", s_gtmp[km])
                        tok = S.done(g.tensor_tensor(
                            gtmp[:, 0:TT],
                            gB[:, cu:cu + TT], uB[:, cu:cu + TT],
                            ALU.mult), "pool")
                        S.mw(tok, s_gtmp[km])
                        S.mr(tok, s_g[ec % NUG], s_u[ec % NUG])
                    block.gpsimd(g2a)

                    def g2(v, ec=ec):
                        cu = (ec % NUG) * TT
                        km = 0
                        S.br(v, "dve", s_gtmp[km], s_u[ec % NUG])
                        S.bw(v, "dve", s_u[ec % NUG])
                        tok = S.done(v.tensor_tensor(
                            uB[:, cu:cu + TT], uB[:, cu:cu + TT],
                            gtmp[:, 0:TT],
                            ALU.subtract), "dve")
                        S.mw(tok, s_u[ec % NUG])
                        S.mr(tok, s_gtmp[km])
                    block.vector(g2)

                    def v3(v, ec=ec, i=i):
                        cu = (ec % NUG) * TT
                        kh = ec % 2
                        S.br(v, "dve", s_u[ec % NUG], s_g[ec % NUG],
                             s_carry[ec])
                        S.bw(v, "dve", s_h32[kh])
                        tok = S.done(v.tensor_tensor_scan(
                            h32[:, kh * TT:(kh + 1) * TT],
                            gB[:, cu:cu + TT], uB[:, cu:cu + TT],
                            carry[:, ec:ec + 1],
                            ALU.mult, ALU.add), "dve")
                        S.mw(tok, s_h32[kh])
                        S.mr(tok, s_u[ec % NUG], s_g[ec % NUG], s_carry[ec])
                        S.bw(v, "dve", s_carry[ec])
                        tok2 = S.done(v.tensor_copy(
                            carry[:, ec:ec + 1],
                            h32[:, (kh + 1) * TT - 1:(kh + 1) * TT]), "dve")
                        S.mw(tok2, s_carry[ec])
                        S.mr(tok2, s_h32[kh])
                    block.vector(v3)

                    def g3(g, ec=ec, i=i):
                        kh = ec % 2
                        pr = i % 2
                        S.br(g, "pool", s_h32[kh])
                        S.bw(g, "pool", s_hbf[pr])
                        tok = S.done(g.tensor_copy(
                            hbf[:, pr * DC * TT + ec * TT:
                                pr * DC * TT + (ec + 1) * TT],
                            h32[:, kh * TT:(kh + 1) * TT]), "pool")
                        if ec == DC - 1:
                            S.mw(tok, s_hbf[pr])
                        TK[("A", i, "hbf", ec)] = tok
                        S.mr(tok, s_h32[kh])
                    block.gpsimd(g3)

            def s2a_fills(i):
                pr = i % 2
                for dch in range(2):
                    def pe3(t, dch=dch):
                        S.wait(t, "pe", TK["woT"])
                        for dc in range(dch * 4, dch * 4 + 4):
                            bk = dc % 4
                            S.bw(t, "pe", s_psA[bk])
                            for ec in range(DC):
                                S.wait(t, "pe", TK[("A", i, "hbf", ec)])
                                inst = t.matmul(
                                    psA[bk][:],
                                    woT[:, ec * D + dc * P: ec * D + (dc + 1) * P],
                                    hbf[:, pr * DC * TT + ec * TT:
                                        pr * DC * TT + (ec + 1) * TT],
                                    start=(ec == 0), stop=(ec == DC - 1))
                            tok = S.done(inst, "pe")
                            S.mw(tok, s_psA[bk])
                            S.mr(tok, s_hbf[pr])
                    block.tensor(pe3)

                    def a5(a, dch=dch):
                        for dc in range(dch * 4, dch * 4 + 4):
                            bk = dc % 4
                            S.bw(a, "act", s_aof[dc])
                            S.br(a, "act", s_psA[bk])
                            tok = S.done(a.activation(
                                aof[:, dc * TT:(dc + 1) * TT], psA[bk][:],
                                AF.Identity, bias=bo_s[:, dc:dc + 1]), "act")
                            S.mw(tok, s_aof[dc])
                            S.mr(tok, s_psA[bk])
                    block.scalar(a5)

            def mk_ao_jobs(i):
                jobs = []
                for tc in range(PCT):
                    for half in range(2):
                        def job(tc=tc, half=half):
                            bk = half

                            def pe4(t):
                                S.wait(t, "pe", TK["idf"])
                                S.bw(t, "pe", s_psTF[bk])
                                dcs = range(half * 4, half * 4 + 4)
                                S.br(t, "pe", *[s_aof[dc] for dc in dcs])
                                for j, dc in enumerate(dcs):
                                    inst = t.transpose(
                                        psTF[bk][:, j * P:(j + 1) * P],
                                        aof[:, dc * TT + tc * P:
                                            dc * TT + tc * P + P],
                                        identf[:])
                                tok = S.done(inst, "pe")
                                S.mw(tok, s_psTF[bk])
                                for dc in dcs:
                                    S.mr(tok, s_aof[dc])
                            block.tensor(pe4)

                            def v4(v):
                                S.br(v, "dve", s_psTF[bk])
                                if half == 0:
                                    S.bw(v, "dve", s_aot[tc])
                                tok = S.done(v.tensor_copy(
                                    aot[:, tc * D + half * 512:
                                        tc * D + (half + 1) * 512],
                                    psTF[bk][:]), "dve")
                                if half == 1:
                                    S.mw(tok, s_aot[tc])
                                S.mr(tok, s_psTF[bk])
                            block.vector(v4)
                        jobs.append(job)
                return jobs

            def s2a_rest(i):
                t0 = i * TT
                emit_ln_batch(("A2", i),
                              [(tc, aot[:, tc * D:(tc + 1) * D], [s_aot[tc]])
                               for tc in range(PCT)], grp=1)

                def a6(v):
                    for tc in range(PCT):
                        S.wait(v, "dve", TK[(("A2", i), "ln", tc)])
                        S.bw(v, "dve", s_aot[tc])
                        q = 4 + tc
                        tok = S.done(v.tensor_scalar(
                            aot[:, tc * D:(tc + 1) * D],
                            aot[:, tc * D:(tc + 1) * D],
                            mv[:, q * 2:q * 2 + 1],
                            rstd[:, q:q + 1],
                            ALU.subtract, ALU.mult), "dve")
                        S.mw(tok, s_aot[tc])
                        S.mr(tok, s_stats[q])
                block.vector(a6)

                kxs = [(i * PCT + tc) % NXT for tc in range(PCT)]

                def v5(g):
                    for tc in range(PCT):
                        kx = kxs[tc]
                        S.br(g, "pool", s_aot[tc], s_xt[kx])
                        S.bw(g, "pool", s_xt[kx])
                        tok = S.done(g.tensor_tensor(
                            xt[:, kx * D:(kx + 1) * D],
                            aot[:, tc * D:(tc + 1) * D],
                            xt[:, kx * D:(kx + 1) * D], ALU.add), "pool")
                        S.mw(tok, s_xt[kx])
                        S.mr(tok, s_aot[tc])
                block.gpsimd(v5)

                def a7(a):
                    for tc in range(PCT):
                        kx = kxs[tc]
                        S.br(a, "act", s_xt[kx])
                        tok = S.dma_done(a.dma_start(
                            out=x1s[t0 + tc * P: t0 + (tc + 1) * P, :],
                            in_=xt[:, kx * D:(kx + 1) * D]), f"x1w{tc}")
                        S.mr(tok, s_xt[kx])
                block.scalar(a7)

                emit_ln_batch(("A3", i),
                              [(tc, xt[:, kxs[tc] * D:(kxs[tc] + 1) * D],
                                [s_xt[kxs[tc]]])
                               for tc in range(PCT)], grp=2)

                def v6(v):
                    S.wait(v, "dve",
                           *[TK[(("A3", i), "ln", tc)] for tc in range(PCT)])
                    for tc in range(PCT):
                        pcid = i * PCT + tc
                        q = 8 + tc
                        v.tensor_copy(m3_all[:, pcid:pcid + 1],
                                      mv[:, q * 2:q * 2 + 1])
                        tok = S.done(v.tensor_copy(
                            rs3_all[:, pcid:pcid + 1],
                            rstd[:, q:q + 1]), "dve")
                        TK[("A", i, "st3", tc)] = tok
                        S.mr(tok, s_stats[q])
                block.vector(v6)

                def a8(v):
                    for tc in range(PCT):
                        kx = kxs[tc]
                        S.bw(v, "dve", s_ln3b[tc])
                        S.br(v, "dve", s_xt[kx])
                        S.wait(v, "dve", TK[(("A3", i), "ln", tc)])
                        q = 8 + tc
                        tok = S.done(v.tensor_scalar(
                            ln3b[:, tc * D:(tc + 1) * D],
                            xt[:, kx * D:(kx + 1) * D],
                            mv[:, q * 2:q * 2 + 1],
                            rstd[:, q:q + 1],
                            ALU.subtract, ALU.mult), "dve")
                        S.mw(tok, s_ln3b[tc])
                        S.mr(tok, s_xt[kx], s_stats[q])
                block.vector(a8)

            def stage2b(i):
                t0 = i * TT
                for half in range(2):
                    for sub in range(2):
                        dcg = half * 2 + sub

                        def pe5(t, dcg=dcg, sub=sub):
                            S.bw(t, "pe", s_psTB[sub])
                            S.br(t, "pe", *s_ln3b)
                            for j in range(2):
                                dc = dcg * 2 + j
                                for tc in range(PCT):
                                    inst = t.transpose(
                                        psTB[sub][:, (j * PCT + tc) * P:
                                                   (j * PCT + tc + 1) * P],
                                        ln3b[:, tc * D + dc * P:
                                             tc * D + (dc + 1) * P],
                                        identb[:])
                            tok = S.done(inst, "pe")
                            S.mw(tok, s_psTB[sub])
                            S.mr(tok, *s_ln3b)
                        block.tensor(pe5)

                    def a9(a, half=half, i=i):
                        for sub in range(2):
                            dcg = half * 2 + sub
                            S.bw(a, "act", s_ln3f)
                            S.br(a, "act", s_psTB[sub])
                            tok = S.done(a.copy(
                                ln3f[:, sub * 2 * TT:(sub + 1) * 2 * TT],
                                psTB[sub][:]), "act")
                            TK[("A", i, "l3f", half)] = tok
                            S.mr(tok, s_psTB[sub])
                    block.scalar(a9)

                    def q2(sync, i=i, half=half):
                        S.wait(sync, "sp", TK[("A", i, "l3f", half)])
                        s_ln3f.w = TK[("A", i, "l3f", half)]
                        tok = S.dma_done(sync.dma_start(
                            out=ln3s[half * 4:(half + 1) * 4, :, t0:t0 + TT]
                            .rearrange("dc p t -> p dc t"),
                            in_=ln3f[:].rearrange("p (dc t) -> p dc t", t=TT)),
                            "ln3w")
                        S.mr(tok, s_ln3f)
                    block.sync(q2)

                    def v7(v, i=i, half=half):
                        S.wait(v, "dve", TK[("A", i, "l3f", half)])
                        S.bw(v, "dve", s_tmp1[0])
                        S.br(v, "dve", s_rsum)
                        t1 = S.done(v.reduce_sum(
                            tmp1[:, 0:4],
                            ln3f[:].rearrange("p (dc t) -> p dc t", t=TT),
                            axis=AX.X), "dve")
                        S.wait(v, "dve", t1)
                        t2 = S.done(v.tensor_tensor(
                            rsum[:, half * 4:(half + 1) * 4],
                            rsum[:, half * 4:(half + 1) * 4],
                            tmp1[:, 0:4], ALU.add), "dve")
                        S.mw(t2, s_rsum)
                        S.mr(t2, s_ln3f, s_tmp1[0])
                        TK[("A", i, "rsum")] = t2
                    block.vector(v7)

            stage1a(0)
            stage1a(1)
            stage1(0)
            for i in range(NT):
                s2a_fills(i)
                if i + 2 < NT:
                    stage1a(i + 2)
                if i + 1 < NT:
                    stage1(i + 1, mk_ao_jobs(i))
                else:
                    for job in mk_ao_jobs(i):
                        job()
                if i - 1 >= 0:
                    stage2b(i - 1)
                s2a_rest(i)
            stage2b(NT - 1)

        S.barrier(block)

        with ExitStack() as stB:
            def sbb(name, shape, dt=F32):
                return stB.enter_context(nc.sbuf_tensor(name, shape, dt))
            rlog = sbb("rlog", [1, E]); rnmx = sbb("rnmx", [1, 1])
            rexp = sbb("rexp", [1, E]); rz1 = sbb("rz1", [1, 1])
            rw_t = sbb("rw_t", [1, E]); rmx1 = sbb("rmx1", [1, 1])
            rmx2 = sbb("rmx2", [1, 1]); msk1 = sbb("msk1", [1, E])
            msk2 = sbb("msk2", [1, E]); rtmp = sbb("rtmp", [1, E])
            cw_row = sbb("cw_row", [1, E])
            e2t = sbb("e2t", [1, 1]); z2t = sbb("z2t", [1, 1])
            rz2 = sbb("rz2", [1, 1]); w2t = sbb("w2t", [1, 1])
            io8 = stB.enter_context(nc.sbuf_tensor("io8", [1, E], I32))
            io8f = sbb("io8f", [1, E]); seltmp = sbb("seltmp", [1, E])
            idxf = sbb("idxf", [1, 2])
            idxi = stB.enter_context(nc.sbuf_tensor("idxi", [1, 2], I32))
            cw2row = sbb("cw2row", [1, 2]); cwk = sbb("cwk", [P, 2])

            w1_sb = sbb("w1_sb", [P, DC * F], BF16)
            w2_sb = sbb("w2_sb", [P, FC * D], BF16)
            b1t_sb = sbb("b1t_sb", [P, FC]); b2t_sb = sbb("b2t_sb", [P, DC])
            ln3t = sbb("ln3t", [P, DC * TT], BF16)
            hB = sbb("hB", [P, FC * TT], BF16)
            outB = sbb("outB", [P, DC * TT])

            psRscope = ExitStack()
            psR = psRscope.enter_context(nc.psum_tensor("psR", [1, E], F32))
            psBC = psRscope.enter_context(nc.psum_tensor("psBC", [P, E], F32))

            s_w1 = Slot(); s_w2 = Slot(); s_b12 = Slot()
            s_ln3t = Slot()
            s_hB = [Slot() for _ in range(FC)]
            s_outB = [Slot() for _ in range(DC)]
            s_psH = [Slot() for _ in range(4)]
            s_psO = [Slot() for _ in range(3)]

            def g_io(g):
                TK["io8"] = S.done(g.iota(io8[:], [[1, E]],
                                          channel_multiplier=0), "pool")
            block.gpsimd(g_io)

            def v8(v):
                S.wait(v, "dve", TK[("A", NT - 1, "rsum")])
                tok = S.done(v.tensor_scalar(
                    rsum[:], rsum[:], 1.0 / T, None, ALU.mult), "dve")
                TK["rmean"] = tok
            block.vector(v8)

            def pe6(t):
                S.wait(t, "pe", TK["rmean"])
                for dc in range(DC):
                    inst = t.matmul(
                        psR[:], rsum[:, dc:dc + 1],
                        wrt_s[:, dc * E:(dc + 1) * E],
                        start=(dc == 0), stop=(dc == DC - 1))
                TK["rlogm"] = S.done(inst, "pe")
            block.tensor(pe6)

            def v9(v):
                S.wait(v, "dve", TK["rlogm"])
                t = S.done(v.tensor_tensor(rlog[:], psR[:], br_s[:], ALU.add), "dve")
                S.wait(v, "dve", t)
                t = S.done(v.reduce_max(rnmx[:], rlog[:], axis=AX.X,
                                        negate=True), "dve")
                TK["rnmx"] = t
            block.vector(v9)

            def a10(a):
                S.wait(a, "act", TK["rnmx"])
                TK["rexp"] = S.done(a.activation(
                    rexp[:], rlog[:], AF.Exp, bias=rnmx[:]), "act")
            block.scalar(a10)

            def v10(v):
                S.wait(v, "dve", TK["rexp"])
                t = S.done(v.reduce_sum(rz1[:], rexp[:], axis=AX.X), "dve")
                S.wait(v, "dve", t)
                t = S.done(v.reciprocal(rz1[:], rz1[:]), "dve")
                S.wait(v, "dve", t)
                t = S.done(v.tensor_scalar(
                    rw_t[:], rexp[:], rz1[:], None, ALU.mult), "dve")
                S.wait(v, "dve", t)
                t = S.done(v.reduce_max(rmx1[:], rw_t[:], axis=AX.X), "dve")
                S.wait(v, "dve", t)
                t = S.done(v.tensor_scalar(
                    msk1[:], rw_t[:], rmx1[:], None, ALU.is_equal), "dve")
                S.wait(v, "dve", t)
                t = S.done(v.tensor_scalar(
                    rtmp[:], msk1[:], 1e30, None, ALU.mult), "dve")
                S.wait(v, "dve", t)
                t = S.done(v.tensor_tensor(
                    rtmp[:], rw_t[:], rtmp[:], ALU.subtract), "dve")
                S.wait(v, "dve", t)
                t = S.done(v.reduce_max(rmx2[:], rtmp[:], axis=AX.X), "dve")
                S.wait(v, "dve", t)
                t = S.done(v.tensor_scalar(
                    msk2[:], rw_t[:], rmx2[:], None, ALU.is_equal), "dve")
                S.wait(v, "dve", t)
                t = S.done(v.tensor_tensor(
                    e2t[:], rmx2[:], rmx1[:], ALU.subtract), "dve")
                TK["e2t"] = t
            block.vector(v10)

            def a11(a):
                S.wait(a, "act", TK["e2t"])
                TK["e2e"] = S.done(a.activation(e2t[:], e2t[:], AF.Exp), "act")
            block.scalar(a11)

            def v11(v):
                S.wait(v, "dve", TK["e2e"])
                t = S.done(v.tensor_scalar(
                    z2t[:], e2t[:], 1.0, None, ALU.add), "dve")
                S.wait(v, "dve", t)
                t = S.done(v.reciprocal(rz2[:], z2t[:]), "dve")
                S.wait(v, "dve", t)
                t = S.done(v.tensor_tensor(w2t[:], e2t[:], rz2[:], ALU.mult), "dve")
                S.wait(v, "dve", t)
                S.wait(v, "dve", TK["io8"])
                t = S.done(v.tensor_copy(io8f[:], io8[:]), "dve")
                S.wait(v, "dve", t)
                t = S.done(v.tensor_tensor(
                    seltmp[:], msk1[:], io8f[:], ALU.mult), "dve")
                S.wait(v, "dve", t)
                t = S.done(v.reduce_sum(idxf[:, 0:1], seltmp[:], axis=AX.X), "dve")
                S.wait(v, "dve", t)
                t = S.done(v.tensor_tensor(
                    seltmp[:], msk2[:], io8f[:], ALU.mult), "dve")
                S.wait(v, "dve", t)
                t = S.done(v.reduce_sum(idxf[:, 1:2], seltmp[:], axis=AX.X), "dve")
                S.wait(v, "dve", t)
                TK["idxi"] = S.done(v.tensor_copy(idxi[:], idxf[:]), "dve")
                v.tensor_copy(cw2row[:, 0:1], rz2[:])
                TK["cw2"] = S.done(v.tensor_copy(cw2row[:, 1:2], w2t[:]), "dve")
                t = S.done(v.tensor_scalar(
                    msk1[:], msk1[:], rz2[:], None, ALU.mult), "dve")
                t = S.done(v.tensor_scalar(
                    msk2[:], msk2[:], w2t[:], None, ALU.mult), "dve")
                S.wait(v, "dve", t)
                t = S.done(v.tensor_tensor(
                    cw_row[:], msk1[:], msk2[:], ALU.add), "dve")
                TK["cw"] = t
            block.vector(v11)

            def pe7(t):
                S.wait(t, "pe", TK["cw"], TK["cw2"])
                TK["cwm"] = S.done(t.matmul(
                    psBC[:], ones1f[:], cw_row[:], start=True, stop=True), "pe")
            block.tensor(pe7)

            def v12(v):
                S.wait(v, "dve", TK["cwm"])
                TK["cwb"] = S.done(v.tensor_copy(cwb[:], psBC[:]), "dve")
            block.vector(v12)

            def pe7b(t):
                S.wait(t, "pe", TK["cwb"])
                TK["cwm2"] = S.done(t.matmul(
                    psBC[:, 0:2], ones1f[:], cw2row[:], start=True, stop=True),
                    "pe")
            block.tensor(pe7b)

            def v12b(v):
                S.wait(v, "dve", TK["cwm2"])
                TK["cwk"] = S.done(v.tensor_copy(cwk[:], psBC[:, 0:2]), "dve")
            block.vector(v12b)

            psRscope.close()
            psH = [stB.enter_context(nc.psum_tensor(f"psH{i}", [P, TT], F32))
                   for i in range(4)]
            psO = [stB.enter_context(nc.psum_tensor(f"psO{i}", [P, TT], F32))
                   for i in range(3)]

            NEXP = 2 if SPARSE else E
            REGS = {}
            for e in range(NEXP):
                if SPARSE:
                    def gw(g, e=e):
                        if "r_idx" not in REGS:
                            REGS["r_idx"] = g.alloc_register("r_idx")
                            REGS["r_off"] = g.alloc_register("r_off")
                        r_idx, r_off = REGS["r_idx"], REGS["r_off"]
                        S.wait(g, "pool", TK["idxi"])
                        g.reg_load(r_idx, idxi[0:1, e:e + 1])
                        S.bw(g, "pool", s_w1)
                        for j in range(2):
                            g.reg_mul(r_off, r_idx, D * F)
                            g.reg_add(r_off, r_off, j * 4 * P * F)
                            tok = S.dma_done(g.dma_start(
                                out=w1_sb[:, j * 4 * F:(j + 1) * 4 * F]
                                .rearrange("p (dc f) -> p dc f", f=F),
                                in_=bass.AP(dh["W1"], r_off,
                                            [[F, P], [P * F, 4], [1, F]])),
                                "w1")
                        S.mw(tok, s_w1)
                        TK[("B", e, "w1")] = tok
                        S.bw(g, "pool", s_w2)
                        for j in range(2):
                            g.reg_mul(r_off, r_idx, F * D)
                            g.reg_add(r_off, r_off, j * 16 * P * D)
                            tok = S.dma_done(g.dma_start(
                                out=w2_sb[:, j * 16 * D:(j + 1) * 16 * D]
                                .rearrange("p (fc d) -> p fc d", d=D),
                                in_=bass.AP(dh["W2"], r_off,
                                            [[D, P], [P * D, 16], [1, D]])),
                                "w2")
                        S.mw(tok, s_w2)
                        TK[("B", e, "w2")] = tok
                        S.bw(g, "pool", s_b12)
                        g.reg_mul(r_off, r_idx, P * FC)
                        S.dma_done(g.dma_start(
                            out=b1t_sb[:],
                            in_=bass.AP(dh["b1t"], r_off, [[FC, P], [1, FC]])),
                            "bld")
                        g.reg_mul(r_off, r_idx, P * DC)
                        tok = S.dma_done(g.dma_start(
                            out=b2t_sb[:],
                            in_=bass.AP(dh["b2t"], r_off, [[DC, P], [1, DC]])),
                            "bld")
                        S.mw(tok, s_b12)
                        TK[("B", e, "b12")] = tok
                    block.gpsimd(gw)
                else:
                    def gw(g, e=e):
                        S.bw(g, "pool", s_w1)
                        for dc in range(DC):
                            tok = S.dma_done(g.dma_start(
                                out=w1_sb[:, dc * F:(dc + 1) * F],
                                in_=d["W1"][e, dc * P:(dc + 1) * P, :]), "w1")
                        S.mw(tok, s_w1)
                        TK[("B", e, "w1")] = tok
                        S.bw(g, "pool", s_w2)
                        for fc in range(FC):
                            tok = S.dma_done(g.dma_start(
                                out=w2_sb[:, fc * D:(fc + 1) * D],
                                in_=d["W2"][e, fc * P:(fc + 1) * P, :]), "w2")
                        S.mw(tok, s_w2)
                        TK[("B", e, "w2")] = tok
                    block.gpsimd(gw)

                    def qb(sync, e=e):
                        S.bw(sync, "sp", s_b12)
                        S.dma_done(sync.dma_start(
                            out=b1t_sb[:], in_=d["b1t"][e]), "bld")
                        tok = S.dma_done(sync.dma_start(
                            out=b2t_sb[:], in_=d["b2t"][e]), "bld")
                        S.mw(tok, s_b12)
                        TK[("B", e, "b12")] = tok
                    block.sync(qb)

                for i in range(NT):
                    t0 = i * TT

                    def q3(sync, i=i, e=e):
                        S.bw(sync, "sp", s_ln3t)
                        tok = S.dma_done(sync.dma_start(
                            out=ln3t[:].rearrange("p (dc t) -> p dc t", t=TT),
                            in_=ln3s[:, :, t0:t0 + TT].rearrange(
                                "dc p t -> p dc t")), "ln3r")
                        S.mw(tok, s_ln3t)
                    block.sync(q3)

                    for fcg in range(FC // 4):
                        def pe8(t, fcg=fcg, e=e):
                            S.wait(t, "pe", TK[("B", e, "w1")], TK["cwk"])
                            S.br(t, "pe", s_ln3t)
                            for fc in range(fcg * 4, fcg * 4 + 4):
                                bk = fc % 4
                                S.bw(t, "pe", s_psH[bk])
                                for dc in range(DC):
                                    inst = t.matmul(
                                        psH[bk][:],
                                        w1_sb[:, dc * F + fc * P:
                                              dc * F + (fc + 1) * P],
                                        ln3t[:, dc * TT:(dc + 1) * TT],
                                        start=(dc == 0), stop=(dc == DC - 1))
                                tok = S.done(inst, "pe")
                                S.mw(tok, s_psH[bk])
                                S.mr(tok, s_ln3t, s_w1)
                        block.tensor(pe8)

                        def a12(a, fcg=fcg, e=e):
                            S.wait(a, "act", TK[("B", e, "b12")])
                            for fc in range(fcg * 4, fcg * 4 + 4):
                                bk = fc % 4
                                S.bw(a, "act", s_hB[fc])
                                S.br(a, "act", s_psH[bk])
                                tok = S.done(a.activation(
                                    hB[:, fc * TT:(fc + 1) * TT], psH[bk][:],
                                    AF.Gelu, bias=b1t_sb[:, fc:fc + 1]), "act")
                                S.mw(tok, s_hB[fc])
                                S.mr(tok, s_psH[bk], s_b12)
                        block.scalar(a12)

                    for dcg in ((0, 1, 2), (3, 4, 5), (6, 7)):
                        def pe9(t, dcg=dcg, e=e):
                            S.wait(t, "pe", TK[("B", e, "w2")])
                            S.br(t, "pe", *s_hB)
                            for dc in dcg:
                                bk = dc % 3
                                S.bw(t, "pe", s_psO[bk])
                                for fc in range(FC):
                                    inst = t.matmul(
                                        psO[bk][:],
                                        w2_sb[:, fc * D + dc * P:
                                              fc * D + (dc + 1) * P],
                                        hB[:, fc * TT:(fc + 1) * TT],
                                        start=(fc == 0), stop=(fc == FC - 1))
                                tok = S.done(inst, "pe")
                                S.mw(tok, s_psO[bk])
                                S.mr(tok, s_w2, *s_hB)
                        block.tensor(pe9)

                        def v13(v, dcg=dcg, e=e):
                            S.wait(v, "dve", TK["cwb"], TK["cwk"],
                                   TK[("B", e, "b12")])
                            for dc in dcg:
                                bk = dc % 3
                                S.bw(v, "dve", s_outB[dc])
                                S.br(v, "dve", s_psO[bk])
                                scale_col = (cwk[:, e:e + 1] if SPARSE
                                             else cwb[:, e:e + 1])
                                tok = S.done(v.tensor_scalar(
                                    outB[:, dc * TT:(dc + 1) * TT], psO[bk][:],
                                    b2t_sb[:, dc:dc + 1], scale_col,
                                    ALU.add, ALU.mult), "dve")
                                S.mw(tok, s_outB[dc])
                                S.mr(tok, s_psO[bk], s_b12)
                        block.vector(v13)

                    def q4(g, i=i, e=e):
                        S.br(g, "pool", *s_outB)
                        S.wait(g, "pool", ("q_accw", S.cnt["q_accw"]))
                        tok = S.dma_done(g.dma_start(
                            out=accs[:, :, t0:t0 + TT].rearrange(
                                "dc p t -> p dc t"),
                            in_=outB[:].rearrange("p (dc t) -> p dc t", t=TT),
                            accum_op=(ALU.bypass if e == 0 else ALU.add)),
                            "accw")
                        for sl in s_outB:
                            S.mr(tok, sl)
                        TK[("B", e, i, "acc")] = tok
                    block.gpsimd(q4)

        S.barrier(block)

        with ExitStack() as stC:
            def sbc(name, shape, dt=F32):
                return stC.enter_context(nc.sbuf_tensor(name, shape, dt))
            gam_r = sbc("gam_r", [1, D]); bet_r = sbc("bet_r", [1, D])
            gam_b = sbc("gam_b", [P, D]); bet_b = sbc("bet_b", [P, D])
            accf = sbc("accf", [P, 2 * DC * TT])
            x1c = sbc("x1c", [P, 3 * 4 * D])
            yt = sbc("yt", [P, 2 * 4 * D])
            zt = sbc("zt", [P, 2 * 4 * D])
            ot = sbc("ot", [P, 4 * D])
            tmpc = sbc("tmpc", [P, 4 * D])
            st6c = sbc("st6c", [P, 3 * 48])
            mvc = sbc("mvc", [P, 3 * 8]); sdc = sbc("sdc", [P, 3 * 4])
            rsc = sbc("rsc", [P, 3 * 4])

            psC = [stC.enter_context(nc.psum_tensor(f"psC{i}", [P, 4 * P], F32))
                   for i in range(2)]
            psG = stC.enter_context(nc.psum_tensor("psG", [P, D], F32))

            s_accf = [Slot() for _ in range(2)]
            s_x1c = [Slot() for _ in range(3)]
            s_yt = [Slot() for _ in range(2)]
            s_zt = [Slot() for _ in range(2)]
            s_ot = [Slot()]
            s_ot.append(s_ot[0])
            s_tmpc = [Slot()]
            s_psC = [Slot() for _ in range(2)]
            s_statc = [Slot() for _ in range(12)]

            def qc(sync):
                S.dma_done(sync.dma_start(out=gam_r[:], in_=d["gamma"][:]),
                           "cst")
                tok = S.dma_done(sync.dma_start(out=bet_r[:], in_=d["beta"][:]),
                                 "cst")
                TK["gb"] = tok
            block.sync(qc)

            def pec(t):
                S.wait(t, "pe", TK["gb"])
                t.matmul(psG[:, 0:512], ones1f[:], gam_r[:, 0:512],
                         start=True, stop=True)
                TK["gbm"] = S.done(t.matmul(
                    psG[:, 512:1024], ones1f[:], gam_r[:, 512:1024],
                    start=True, stop=True), "pe")
            block.tensor(pec)

            def vc(v):
                S.wait(v, "dve", TK["gbm"])
                TK["gbe"] = S.done(v.tensor_copy(gam_b[:], psG[:]), "dve")
            block.vector(vc)

            def pec2(t):
                S.wait(t, "pe", TK["gbe"])
                t.matmul(psG[:, 0:512], ones1f[:], bet_r[:, 0:512],
                         start=True, stop=True)
                TK["bbm"] = S.done(t.matmul(
                    psG[:, 512:1024], ones1f[:], bet_r[:, 512:1024],
                    start=True, stop=True), "pe")
            block.tensor(pec2)

            def vc2(v):
                S.wait(v, "dve", TK["bbm"])
                TK["bbe"] = S.done(v.tensor_copy(bet_b[:], psG[:]), "dve")
            block.vector(vc2)

            def emit_ln_c(tag, mk_src, grp):
                def v1(v):
                    for tc in range(PCT):
                        src_ap, slots = mk_src(tc)
                        q = grp * 4 + tc
                        S.bw(v, "dve", s_statc[q])
                        S.br(v, "dve", *slots)
                        v.bn_stats(st6c[:, q * 12:q * 12 + 6],
                                   src_ap[:, 0:512])
                        t = S.done(v.bn_stats(
                            st6c[:, q * 12 + 6:q * 12 + 12],
                            src_ap[:, 512:1024]), "dve")
                        S.wait(v, "dve", t)
                        TK[(tag, "mv", tc)] = S.done(v.bn_aggr(
                            mvc[:, q * 2:q * 2 + 2],
                            st6c[:, q * 12:(q + 1) * 12]), "dve")
                block.vector(v1)

                def a1(a):
                    S.wait(a, "act",
                           *[TK[(tag, "mv", tc)] for tc in range(PCT)])
                    for tc in range(PCT):
                        q = grp * 4 + tc
                        tok = S.done(a.activation(
                            sdc[:, q:q + 1], mvc[:, q * 2 + 1:q * 2 + 2],
                            AF.Sqrt, bias=eps_t[:]), "act")
                    TK[(tag, "sd")] = tok
                block.scalar(a1)

                def v2(v):
                    S.wait(v, "dve", TK[(tag, "sd")])
                    for tc in range(PCT):
                        q = grp * 4 + tc
                        tok = S.done(v.reciprocal(
                            rsc[:, q:q + 1], sdc[:, q:q + 1]), "dve")
                        S.mw(tok, s_statc[q])
                        _, slots = mk_src(tc)
                        for ss in slots:
                            S.mr(tok, ss)
                        TK[(tag, "ln", tc)] = tok
                block.vector(v2)

            def stageC1(i):
                t0 = i * TT
                ki = i % 2

                def q5(sync, i=i, ki=ki, t0=t0):
                    S.bw(sync, "sp", s_accf[ki])
                    tok = S.dma_done(sync.dma_start(
                        out=accf[:, ki * DC * TT:(ki + 1) * DC * TT].rearrange(
                            "p (dc t) -> p dc t", t=TT),
                        in_=accs[:, :, t0:t0 + TT].rearrange("dc p t -> p dc t")),
                        f"acc{ki}")
                    S.mw(tok, s_accf[ki])
                    kx = i % 3
                    S.bw(sync, "sp", s_x1c[kx])
                    tok = S.dma_done(sync.dma_start(
                        out=x1c[:, kx * 4 * D:(kx + 1) * 4 * D].rearrange(
                            "p (tc d) -> p tc d", d=D),
                        in_=x1s.rearrange("(n tc p) d -> n p tc d",
                                          p=P, tc=PCT)[i]), f"x1c{kx}")
                    S.mw(tok, s_x1c[kx])
                block.sync(q5)

                for tc in range(PCT):
                    for half in range(2):
                        bk = half

                        def pe10(t, i=i, ki=ki, tc=tc, half=half, bk=bk):
                            S.wait(t, "pe", TK["idf"])
                            S.bw(t, "pe", s_psC[bk])
                            S.br(t, "pe", s_accf[ki])
                            for j in range(4):
                                dc = half * 4 + j
                                inst = t.transpose(
                                    psC[bk][:, j * P:(j + 1) * P],
                                    accf[:, ki * DC * TT + dc * TT + tc * P:
                                         ki * DC * TT + dc * TT + tc * P + P],
                                    identf[:])
                            tok = S.done(inst, "pe")
                            S.mw(tok, s_psC[bk])
                            S.mr(tok, s_accf[ki])
                        block.tensor(pe10)

                    def ac0(a, ki=ki, tc=tc):
                        if tc == 0:
                            S.bw(a, "act", s_yt[ki])
                        for half in range(2):
                            bk = half
                            S.br(a, "act", s_psC[bk])
                            tok = S.done(a.copy(
                                yt[:, ki * 4 * D + tc * D + half * 512:
                                   ki * 4 * D + tc * D + (half + 1) * 512],
                                psC[bk][:]), "act")
                            S.mr(tok, s_psC[bk])
                        if tc == PCT - 1:
                            S.mw(tok, s_yt[ki])
                    block.scalar(ac0)

            def stageC2a(i):
                ki = i % 2
                kx = i % 3
                def vc4(v, i=i, ki=ki, kx=kx):
                    S.bw(v, "dve", s_tmpc[0])
                    S.br(v, "dve", s_x1c[kx])
                    for tc in range(PCT):
                        pcid = i * PCT + tc
                        S.wait(v, "dve", TK[("A", i, "st3", tc)])
                        tok = S.done(v.tensor_scalar(
                            tmpc[:, tc * D:(tc + 1) * D],
                            x1c[:, kx * 4 * D + tc * D:
                                kx * 4 * D + (tc + 1) * D],
                            m3_all[:, pcid:pcid + 1],
                            rs3_all[:, pcid:pcid + 1],
                            ALU.subtract, ALU.mult), "dve")
                    S.mw(tok, s_tmpc[0])
                    S.mr(tok, s_x1c[kx])
                block.vector(vc4)

                def gc1(v, ki=ki):
                    S.br(v, "dve", s_tmpc[0], s_yt[ki])
                    S.bw(v, "dve", s_yt[ki])
                    for tc in range(PCT):
                        tok = S.done(v.tensor_tensor(
                            yt[:, ki * 4 * D + tc * D:
                               ki * 4 * D + (tc + 1) * D],
                            yt[:, ki * 4 * D + tc * D:
                               ki * 4 * D + (tc + 1) * D],
                            tmpc[:, tc * D:(tc + 1) * D], ALU.add), "dve")
                    S.mw(tok, s_yt[ki])
                    S.mr(tok, s_tmpc[0])
                block.vector(gc1)

                emit_ln_c(("Cy", i),
                          lambda tc, ki=ki: (
                              yt[:, ki * 4 * D + tc * D:
                                 ki * 4 * D + (tc + 1) * D], [s_yt[ki]]),
                          grp=ki)

            def stageC2b(i):
                ki = i % 2

                def vc5(v, i=i, ki=ki):
                    S.bw(v, "dve", s_zt[ki])
                    S.br(v, "dve", s_yt[ki])
                    for tc in range(PCT):
                        q = ki * 4 + tc
                        S.wait(v, "dve", TK[(("Cy", i), "ln", tc)])
                        tok = S.done(v.tensor_scalar(
                            zt[:, ki * 4 * D + tc * D:
                               ki * 4 * D + (tc + 1) * D],
                            yt[:, ki * 4 * D + tc * D:
                               ki * 4 * D + (tc + 1) * D],
                            mvc[:, q * 2:q * 2 + 1],
                            rsc[:, q:q + 1],
                            ALU.subtract, ALU.mult), "dve")
                        S.mr(tok, s_statc[q])
                    S.mw(tok, s_zt[ki])
                    S.mr(tok, s_yt[ki])
                block.vector(vc5)

                def gc2(v, ki=ki):
                    S.wait(v, "dve", TK["gbe"], TK["bbe"])
                    S.br(v, "dve", s_zt[ki])
                    S.bw(v, "dve", s_zt[ki])
                    for tc in range(PCT):
                        t1 = S.done(v.tensor_tensor(
                            zt[:, ki * 4 * D + tc * D:
                               ki * 4 * D + (tc + 1) * D],
                            zt[:, ki * 4 * D + tc * D:
                               ki * 4 * D + (tc + 1) * D],
                            gam_b[:], ALU.mult), "dve")
                        S.wait(v, "dve", t1)
                        tok = S.done(v.tensor_tensor(
                            zt[:, ki * 4 * D + tc * D:
                               ki * 4 * D + (tc + 1) * D],
                            zt[:, ki * 4 * D + tc * D:
                               ki * 4 * D + (tc + 1) * D],
                            bet_b[:], ALU.add), "dve")
                    S.mw(tok, s_zt[ki])
                block.vector(gc2)

                emit_ln_c(("Cz", i),
                          lambda tc, ki=ki: (
                              zt[:, ki * 4 * D + tc * D:
                                 ki * 4 * D + (tc + 1) * D], [s_zt[ki]]),
                          grp=2)

            def stageC2c(i):
                t0 = i * TT
                ki = i % 2
                kx = i % 3
                def vc7(v, i=i, ki=ki):
                    S.bw(v, "dve", s_ot[ki])
                    S.br(v, "dve", s_zt[ki])
                    for tc in range(PCT):
                        q = 2 * 4 + tc
                        S.wait(v, "dve", TK[(("Cz", i), "ln", tc)])
                        tok = S.done(v.tensor_scalar(
                            ot[:, tc * D:(tc + 1) * D],
                            zt[:, ki * 4 * D + tc * D:
                               ki * 4 * D + (tc + 1) * D],
                            mvc[:, q * 2:q * 2 + 1],
                            rsc[:, q:q + 1],
                            ALU.subtract, ALU.mult), "dve")
                        S.mr(tok, s_statc[q])
                    S.mw(tok, s_ot[ki])
                    S.mr(tok, s_zt[ki])
                block.vector(vc7)

                def gc3(g, ki=ki, kx=kx):
                    S.br(g, "pool", s_ot[ki], s_x1c[kx])
                    S.bw(g, "pool", s_ot[ki])
                    for tc in range(PCT):
                        tok = S.done(g.tensor_tensor(
                            ot[:, tc * D:(tc + 1) * D],
                            ot[:, tc * D:(tc + 1) * D],
                            x1c[:, kx * 4 * D + tc * D:
                                kx * 4 * D + (tc + 1) * D], ALU.add), "pool")
                    S.mw(tok, s_ot[ki])
                    S.mr(tok, s_x1c[kx])
                block.gpsimd(gc3)

                def ac3(a, i=i, ki=ki, t0=t0):
                    S.br(a, "act", s_ot[ki])
                    for tc in range(PCT):
                        tok = S.dma_done(a.dma_start(
                            out=out_d[t0 + tc * P: t0 + (tc + 1) * P, :],
                            in_=ot[:, tc * D:(tc + 1) * D]),
                            f"outw{ki}")
                        S.mr(tok, s_ot[ki])
                block.scalar(ac3)

            for k in range(NT + 3):
                if 3 <= k:
                    stageC2c(k - 3)
                if 2 <= k < NT + 2:
                    stageC2b(k - 2)
                if 1 <= k < NT + 1:
                    stageC2a(k - 1)
                if k < NT:
                    stageC1(k)

            def qf(sync):
                S.wait(sync, "sp", ("q_outw0", S.cnt["q_outw0"]),
                       ("q_outw1", S.cnt["q_outw1"]))
            block.sync(qf)

    nc.finalize()
    return nc


def kernel(L0_attn_x, L0_attn_Wi, L0_attn_bi, L0_attn_Wg, L0_attn_bg,
           L0_attn_Wo, L0_attn_bo, L0_ffn_W1, L0_ffn_b1, L0_ffn_W2,
           L0_ffn_b2, L0_ffn_Wr, L0_ffn_br, L0_ffn_gamma, L0_ffn_beta):
    x = np.asarray(L0_attn_x, dtype=np.float32)
    B, T, _ = x.shape
    fm = lambda v: np.ascontiguousarray(
        np.asarray(v, dtype=np.float32).reshape(DC, P).T)
    common = {
        "WiT": np.ascontiguousarray(np.asarray(L0_attn_Wi, np.float32).T),
        "WgT": np.ascontiguousarray(np.asarray(L0_attn_Wg, np.float32).T),
        "WoT": np.ascontiguousarray(np.asarray(L0_attn_Wo, np.float32).T),
        "bi": fm(L0_attn_bi), "bg": fm(L0_attn_bg), "bo": fm(L0_attn_bo),
        "W1": np.ascontiguousarray(np.asarray(L0_ffn_W1, np.float32)),
        "b1t": np.ascontiguousarray(
            np.asarray(L0_ffn_b1, np.float32).reshape(E, FC, P)
            .transpose(0, 2, 1)),
        "W2": np.ascontiguousarray(np.asarray(L0_ffn_W2, np.float32)),
        "b2t": np.ascontiguousarray(
            np.asarray(L0_ffn_b2, np.float32).reshape(E, DC, P)
            .transpose(0, 2, 1)),
        "Wrt": np.ascontiguousarray(
            np.asarray(L0_ffn_Wr, np.float32).reshape(DC, P, E)
            .transpose(1, 0, 2)),
        "br": np.asarray(L0_ffn_br, np.float32).reshape(1, E),
        "gamma": np.asarray(L0_ffn_gamma, np.float32).reshape(1, D),
        "beta": np.asarray(L0_ffn_beta, np.float32).reshape(1, D),
    }
    nc = build(T)
    in_maps = [dict(common, x=np.ascontiguousarray(x[b])) for b in range(B)]
    res = run_bass_kernel_spmd(nc, in_maps, core_ids=list(range(B)))
    return np.stack([np.asarray(r["out"], dtype=np.float32)
                     for r in res.results], axis=0)


# version 19
# speedup vs baseline: 1.0375x; 1.0013x over previous
import numpy as np
from contextlib import ExitStack

import concourse.bass as bass
import concourse.mybir as mybir
from concourse.bass_utils import run_bass_kernel_spmd

F32 = mybir.dt.float32
BF16 = mybir.dt.bfloat16
I32 = mybir.dt.int32
AF = mybir.ActivationFunctionType
ALU = mybir.AluOpType
AX = mybir.AxisListType

SPARSE = True

P = 128
D = 1024
DC = D // P
F = 4096
FC = F // P
E = 8
TT = 512
PCT = TT // P
EPS = 1e-5


class Slot:
    __slots__ = ("w", "r")

    def __init__(self):
        self.w = None
        self.r = []


class Sync:

    def __init__(self, nc, stack):
        self.nc = nc
        self.stack = stack
        self.sems = {}
        self.cnt = {}
        for name in ("pe", "act", "dve", "pool", "sp"):
            self._mk(name)
        self.last_wait = {}

    def _mk(self, name):
        self.sems[name] = self.stack.enter_context(self.nc.semaphore("S_" + name))
        self.cnt[name] = 0

    def stream(self, name):
        key = "q_" + name
        if key not in self.sems:
            self._mk(key)
        return key

    def done(self, inst, eng):
        self.cnt[eng] += 1
        inst.then_inc(self.sems[eng], 1)
        return (eng, self.cnt[eng])

    def dma_done(self, inst, stream):
        key = "q_" + stream
        self.cnt[key] += 16
        inst.then_inc(self.sems[key], 16)
        return (key, self.cnt[key])

    def wait(self, eng_obj, waiter, *toks):
        need = {}
        for t in toks:
            if t is None:
                continue
            s, v = t
            if v > need.get(s, -1):
                need[s] = v
        for s, v in sorted(need.items()):
            key = (waiter, s)
            if self.last_wait.get(key, -1) >= v:
                continue
            eng_obj.wait_ge(self.sems[s], v)
            self.last_wait[key] = v

    def bw(self, eng_obj, waiter, *slots):
        toks = []
        for s in slots:
            if s.w is not None:
                toks.append(s.w)
            toks.extend(s.r)
        self.wait(eng_obj, waiter, *toks)

    def br(self, eng_obj, waiter, *slots):
        self.wait(eng_obj, waiter, *[s.w for s in slots if s.w is not None])

    @staticmethod
    def mw(tok, *slots):
        for s in slots:
            s.w = tok
            s.r = []

    @staticmethod
    def mr(tok, *slots):
        for s in slots:
            s.r.append(tok)

    def barrier(self, block, streams=()):
        toks = [(k, v) for k, v in self.cnt.items() if v > 0]
        for name, opener in (("pe", block.tensor), ("act", block.scalar),
                             ("dve", block.vector), ("pool", block.gpsimd),
                             ("sp", block.sync)):
            def f(eng, name=name):
                self.wait(eng, name, *toks)
            opener(f)


def build(T=8192):
    NT = T // TT
    NPC = T // P
    nc = bass.Bass("TRN2", target_bir_lowering=False)

    dt_in = {
        "x": ([T, D], F32), "WiT": ([D, D], F32), "WgT": ([D, D], F32),
        "WoT": ([D, D], F32), "bi": ([P, DC], F32), "bg": ([P, DC], F32),
        "bo": ([P, DC], F32), "W1": ([E, D, F], F32), "b1t": ([E, P, FC], F32),
        "W2": ([E, F, D], F32), "b2t": ([E, P, DC], F32),
        "Wrt": ([P, DC, E], F32), "br": ([1, E], F32),
        "gamma": ([1, D], F32), "beta": ([1, D], F32),
    }
    dh = {k: nc.dram_tensor(k, sh, dt, kind="ExternalInput")
          for k, (sh, dt) in dt_in.items()}
    d = {k: h.ap() for k, h in dh.items()}
    x1s = nc.dram_tensor("x1s", [T, D], F32, kind="Internal").ap()
    ln3s = nc.dram_tensor("ln3s", [DC, P, T], BF16, kind="Internal").ap()
    accs = nc.dram_tensor("accs", [DC, P, T], F32, kind="Internal").ap()
    out_d = nc.dram_tensor("out", [T, D], F32, kind="ExternalOutput").ap()

    st = ExitStack()
    with st:
        S = Sync(nc, st)
        for q in ("cst", "x1w0", "x1w1", "x1w2", "x1w3", "ln3w", "ln3r",
                  "w1", "w2", "bld", "accw", "acc0", "acc1", "x1c0", "x1c1",
                  "x1c2", "outw0", "outw1"):
            S.stream(q)
        for i in range(8):
            S.stream(f"xt{i}")
        block = st.enter_context(nc.Block())

        def sb(name, shape, dt=F32):
            return st.enter_context(nc.sbuf_tensor(name, shape, dt))

        identb = sb("identb", [P, P], BF16)
        identf = sb("identf", [P, P], F32)
        eps_t = sb("eps_t", [P, 1], F32)
        ones1f = sb("ones1f", [1, P], F32)
        bi_s = sb("bi_s", [P, DC]); bg_s = sb("bg_s", [P, DC])
        bgn_s = sb("bgn_s", [P, DC]); bo_s = sb("bo_s", [P, DC])
        wrt_s = sb("wrt_s", [P, DC * E]); br_s = sb("br_s", [1, E])
        carry = sb("carry", [P, DC])
        rsum = sb("rsum", [P, DC])
        m3_all = sb("m3_all", [P, NPC]); rs3_all = sb("rs3_all", [P, NPC])
        cwb = sb("cwb", [P, E])

        TK = {}

        def sync_startup(sync):
            TK["bi"] = S.dma_done(sync.dma_start(out=bi_s[:], in_=d["bi"][:]), "cst")
            S.dma_done(sync.dma_start(out=bg_s[:], in_=d["bg"][:]), "cst")
            S.dma_done(sync.dma_start(out=bo_s[:], in_=d["bo"][:]), "cst")
            S.dma_done(sync.dma_start(
                out=wrt_s[:].rearrange("p (dc e) -> p dc e", e=E),
                in_=d["Wrt"][:]), "cst")
            TK["cst"] = S.dma_done(sync.dma_start(out=br_s[:], in_=d["br"][:]), "cst")
        block.sync(sync_startup)

        with nc.sbuf_tensor("io_r", [P, P], I32) as io_r, \
             nc.sbuf_tensor("io_c", [P, 1], I32) as io_c, \
             nc.sbuf_tensor("io_rf", [P, P], F32) as io_rf, \
             nc.sbuf_tensor("io_cf", [P, 1], F32) as io_cf:

            def g_start(g):
                TK["ior"] = S.done(g.iota(io_r[:], [[1, P]], channel_multiplier=0), "pool")
                TK["ioc"] = S.done(g.iota(io_c[:], [[0, 1]], channel_multiplier=1), "pool")
            block.gpsimd(g_start)

            def v_start(v):
                S.wait(v, "dve", TK["ior"], TK["ioc"])
                t1 = S.done(v.tensor_copy(io_rf[:], io_r[:]), "dve")
                t2 = S.done(v.tensor_copy(io_cf[:], io_c[:]), "dve")
                S.wait(v, "dve", t1, t2)
                t3 = S.done(v.tensor_scalar(
                    identf[:], io_rf[:], io_cf[:], None, ALU.is_equal), "dve")
                S.wait(v, "dve", t3)
                TK["idb"] = S.done(v.tensor_copy(identb[:], identf[:]), "dve")
                TK["idf"] = t3
                S.done(v.memset(eps_t[:], EPS), "dve")
                S.done(v.memset(ones1f[:], 1.0), "dve")
                S.done(v.memset(carry[:], 0.0), "dve")
                TK["z"] = S.done(v.memset(rsum[:], 0.0), "dve")
                S.wait(v, "dve", TK["cst"], TK["bi"])
                TK["bgn"] = S.done(v.tensor_scalar(
                    bgn_s[:], bg_s[:], -1.0, None, ALU.mult), "dve")
            block.vector(v_start)

        S.barrier(block)

        with ExitStack() as stA:
            def sba(name, shape, dt=F32):
                return stA.enter_context(nc.sbuf_tensor(name, shape, dt))
            NXT = 12
            xt = sba("xt", [P, NXT * D])
            wiT = sba("wiT", [P, D * DC], BF16)
            wgT = sba("wgT", [P, D * DC], BF16)
            woT = sba("woT", [P, D * DC], BF16)
            ln1b = sba("ln1b", [P, 8 * D], BF16)
            ln1f = sba("ln1f", [P, DC * TT], BF16)
            NUG = 4
            uB = sba("uB", [P, NUG * TT]); gB = sba("gB", [P, NUG * TT])
            gtmp = sba("gtmp", [P, TT])
            h32 = sba("h32", [P, 2 * TT])
            hbf = sba("hbf", [P, 2 * DC * TT], BF16)
            aof = sba("aof", [P, DC * TT])
            aot = sba("aot", [P, 4 * D])
            ln3b = sba("ln3b", [P, 4 * D], BF16)
            ln3f = sba("ln3f", [P, 4 * TT], BF16)
            st6 = sba("st6", [P, 12 * 12])
            mv = sba("mv", [P, 12 * 2])
            sdv = sba("sdv", [P, 12]); rstd = sba("rstd", [P, 12])
            tmp1 = sba("tmp1", [P, DC])

            psA = [stA.enter_context(nc.psum_tensor(f"psA{i}", [P, TT], F32))
                   for i in range(4)]
            psTB = [stA.enter_context(nc.psum_tensor(f"psTB{i}", [P, 8 * P], BF16))
                    for i in range(2)]
            psTF = [stA.enter_context(nc.psum_tensor(f"psTF{i}", [P, 4 * P], F32))
                    for i in range(2)]

            s_xt = [Slot() for _ in range(NXT)]
            s_ln1b = [Slot() for _ in range(8)]
            s_ln1f = [Slot() for _ in range(DC)]
            s_u = [Slot() for _ in range(NUG)]
            s_g = [Slot() for _ in range(NUG)]
            s_gtmp = [Slot()]
            s_h32 = [Slot() for _ in range(2)]
            s_hbf = [Slot() for _ in range(2)]
            s_aof = [Slot() for _ in range(DC)]
            s_aot = [Slot() for _ in range(4)]
            s_ln3b = [Slot() for _ in range(4)]
            s_ln3f = Slot()
            s_psA = [Slot() for _ in range(4)]
            s_psTB = [Slot() for _ in range(2)]
            s_psTF = [Slot() for _ in range(2)]
            s_carry = [Slot() for _ in range(DC)]
            s_stats = [Slot() for _ in range(12)]
            s_rsum = Slot()
            s_tmp1 = [Slot() for _ in range(2)]

            def g_wload(g):
                S.dma_done(g.dma_start(
                    out=wiT[:].rearrange("p (dc e) -> p dc e", e=D),
                    in_=d["WiT"].rearrange("(dc p) e -> p dc e", p=P)), "w1")
                S.dma_done(g.dma_start(
                    out=wgT[:].rearrange("p (dc e) -> p dc e", e=D),
                    in_=d["WgT"].rearrange("(dc p) e -> p dc e", p=P)), "w1")
                tok = S.dma_done(g.dma_start(
                    out=woT[:].rearrange("p (dc e) -> p dc e", e=D),
                    in_=d["WoT"].rearrange("(dc p) e -> p dc e", p=P)), "w1")
                TK["wiT"] = TK["wgT"] = TK["woT"] = tok
            block.gpsimd(g_wload)

            def ln_chain(v_or_none, tc, src_ap, want_bias=True):
                pass

            def emit_ln_stats(tile_tag, tc, src_ap, src_slots):
                sl = s_stats[tc]

                def v1(v):
                    S.bw(v, "dve", sl)
                    S.br(v, "dve", *src_slots)
                    v.bn_stats(st6[:, tc * 12:tc * 12 + 6], src_ap[:, 0:512])
                    t = S.done(v.bn_stats(
                        st6[:, tc * 12 + 6:tc * 12 + 12], src_ap[:, 512:1024]), "dve")
                    S.wait(v, "dve", t)
                    t2 = S.done(v.bn_aggr(
                        mv[:, tc * 2:tc * 2 + 2], st6[:, tc * 12:(tc + 1) * 12]), "dve")
                    TK[(tile_tag, "mv", tc)] = t2
                block.vector(v1)

                def a1(a):
                    S.wait(a, "act", TK[(tile_tag, "mv", tc)])
                    TK[(tile_tag, "sd", tc)] = S.done(a.activation(
                        sdv[:, tc:tc + 1], mv[:, tc * 2 + 1:tc * 2 + 2],
                        AF.Sqrt, bias=eps_t[:]), "act")
                block.scalar(a1)

                def v2(v):
                    S.wait(v, "dve", TK[(tile_tag, "sd", tc)])
                    t2 = S.done(v.reciprocal(rstd[:, tc:tc + 1],
                                             sdv[:, tc:tc + 1]), "dve")
                    S.mw(t2, sl)
                    for ss in src_slots:
                        S.mr(t2, ss)
                    TK[(tile_tag, "ln", tc)] = t2
                block.vector(v2)

            def emit_ln_batch(tag, specs, grp):
                def v1(v):
                    for tc, src_ap, slots in specs:
                        q = grp * 4 + tc
                        S.bw(v, "dve", s_stats[q])
                        S.br(v, "dve", *slots)
                        v.bn_stats(st6[:, q * 12:q * 12 + 6], src_ap[:, 0:512])
                        t = S.done(v.bn_stats(
                            st6[:, q * 12 + 6:q * 12 + 12],
                            src_ap[:, 512:1024]), "dve")
                        S.wait(v, "dve", t)
                        TK[(tag, "mv", tc)] = S.done(v.bn_aggr(
                            mv[:, q * 2:q * 2 + 2],
                            st6[:, q * 12:(q + 1) * 12]), "dve")
                block.vector(v1)

                def a1(a):
                    S.wait(a, "act",
                           *[TK[(tag, "mv", tc)] for tc, _, _ in specs])
                    for tc, _, _ in specs:
                        q = grp * 4 + tc
                        tok = S.done(a.activation(
                            sdv[:, q:q + 1], mv[:, q * 2 + 1:q * 2 + 2],
                            AF.Sqrt, bias=eps_t[:]), "act")
                    TK[(tag, "sd")] = tok
                block.scalar(a1)

                def v2(v):
                    S.wait(v, "dve", TK[(tag, "sd")])
                    for tc, _, slots in specs:
                        q = grp * 4 + tc
                        tok = S.done(v.reciprocal(
                            rstd[:, q:q + 1], sdv[:, q:q + 1]), "dve")
                        S.mw(tok, s_stats[q])
                        for ss in slots:
                            S.mr(tok, ss)
                        TK[(tag, "ln", tc)] = tok
                block.vector(v2)

            def stage1a(i):
                t0 = i * TT
                lb = (i % 2) * PCT

                def q1(sync):
                    for tc in range(PCT):
                        k = (i * PCT + tc) % NXT
                        S.bw(sync, "sp", s_xt[k])
                        tok = S.dma_done(sync.dma_start(
                            out=xt[:, k * D:(k + 1) * D],
                            in_=d["x"][t0 + tc * P: t0 + (tc + 1) * P, :]),
                            f"xt{k % 8}")
                        S.mw(tok, s_xt[k])
                block.sync(q1)

                ks = [(i * PCT + tc) % NXT for tc in range(PCT)]
                emit_ln_batch(("A", i),
                              [(tc, xt[:, ks[tc] * D:(ks[tc] + 1) * D],
                                [s_xt[ks[tc]]]) for tc in range(PCT)],
                              grp=0)

                def a2(v):
                    for tc in range(PCT):
                        k = ks[tc]
                        S.bw(v, "dve", s_ln1b[lb + tc])
                        S.wait(v, "dve", TK[(("A", i), "ln", tc)])
                        tok = S.done(v.tensor_scalar(
                            ln1b[:, (lb + tc) * D:(lb + tc + 1) * D],
                            xt[:, k * D:(k + 1) * D],
                            mv[:, tc * 2:tc * 2 + 1],
                            rstd[:, tc:tc + 1],
                            ALU.subtract, ALU.mult), "dve")
                        S.mw(tok, s_ln1b[lb + tc])
                        S.mr(tok, s_xt[k], s_stats[tc])
                block.vector(a2)

            def stage1(i, xjobs=()):
                xjobs = list(xjobs)
                t0 = i * TT
                lb = (i % 2) * PCT

                for half in range(2):
                    for sub in range(2):
                        dcg = half * 2 + sub

                        def pe1(t, dcg=dcg, sub=sub, lb=lb):
                            S.wait(t, "pe", TK["idb"])
                            S.bw(t, "pe", s_psTB[sub])
                            for j in range(2):
                                dc = dcg * 2 + j
                                for tc in range(PCT):
                                    S.br(t, "pe", s_ln1b[lb + tc])
                                    inst = t.transpose(
                                        psTB[sub][:, (j * PCT + tc) * P:
                                                   (j * PCT + tc + 1) * P],
                                        ln1b[:, (lb + tc) * D + dc * P:
                                             (lb + tc) * D + (dc + 1) * P],
                                        identb[:])
                            tok = S.done(inst, "pe")
                            S.mw(tok, s_psTB[sub])
                            S.mr(tok, *s_ln1b[lb:lb + PCT])
                        block.tensor(pe1)

                    def a3(a, half=half):
                        for sub in range(2):
                            dcg = half * 2 + sub
                            d0 = dcg * 2
                            S.bw(a, "act", s_ln1f[d0], s_ln1f[d0 + 1])
                            S.br(a, "act", s_psTB[sub])
                            tok = S.done(a.copy(
                                ln1f[:, d0 * TT:(d0 + 2) * TT],
                                psTB[sub][:]), "act")
                            S.mw(tok, s_ln1f[d0], s_ln1f[d0 + 1])
                            S.mr(tok, s_psTB[sub])
                    block.scalar(a3)

                pending = []
                for ecp in range(DC // 2):
                    def pe2(t, ecp=ecp):
                        S.wait(t, "pe", TK["wiT"], TK["wgT"])
                        for ec in (2 * ecp, 2 * ecp + 1):
                            ku = (2 * ec) % 4
                            kg = (2 * ec + 1) % 4
                            for bk, wsb in ((ku, wiT), (kg, wgT)):
                                S.bw(t, "pe", s_psA[bk])
                                for dc in range(DC):
                                    S.br(t, "pe", s_ln1f[dc])
                                    inst = t.matmul(
                                        psA[bk][:],
                                        wsb[:, dc * D + ec * P:
                                            dc * D + (ec + 1) * P],
                                        ln1f[:, dc * TT:(dc + 1) * TT],
                                        start=(dc == 0), stop=(dc == DC - 1))
                                tok = S.done(inst, "pe")
                                S.mw(tok, s_psA[bk])
                                for dc in range(DC):
                                    S.mr(tok, s_ln1f[dc])
                    block.tensor(pe2)

                    def a4(a, ecp=ecp):
                        for ec in (2 * ecp, 2 * ecp + 1):
                            ku = (2 * ec) % 4
                            kg = (2 * ec + 1) % 4
                            cu = (ec % NUG) * TT
                            S.bw(a, "act", s_u[ec % NUG])
                            S.br(a, "act", s_psA[ku])
                            tok = S.done(a.activation(
                                uB[:, cu:cu + TT], psA[ku][:], AF.Identity,
                                bias=bi_s[:, ec:ec + 1]), "act")
                            S.mw(tok, s_u[ec % NUG])
                            S.mr(tok, s_psA[ku])
                            S.bw(a, "act", s_g[ec % NUG])
                            S.br(a, "act", s_psA[kg])
                            tokg = S.done(a.activation(
                                gB[:, cu:cu + TT], psA[kg][:], AF.Sigmoid,
                                bias=bg_s[:, ec:ec + 1]), "act")
                            S.mw(tokg, s_g[ec % NUG])
                            S.mr(tokg, s_psA[kg])
                    block.scalar(a4)

                    for ec in pending:
                        emit_scan_chain(i, ec)
                    pending = [2 * ecp, 2 * ecp + 1]
                    for _ in range(2):
                        if xjobs:
                            xjobs.pop(0)()
                for ec in pending:
                    emit_scan_chain(i, ec)
                for job in xjobs:
                    job()

            def emit_scan_chain(i, ec):
                    def g2a(g, ec=ec):
                        cu = (ec % NUG) * TT
                        km = 0
                        S.br(g, "pool", s_u[ec % NUG], s_g[ec % NUG])
                        S.bw(g, "pool", s_gtmp[km])
                        tok = S.done(g.tensor_tensor(
                            gtmp[:, 0:TT],
                            gB[:, cu:cu + TT], uB[:, cu:cu + TT],
                            ALU.mult), "pool")
                        S.mw(tok, s_gtmp[km])
                        S.mr(tok, s_g[ec % NUG], s_u[ec % NUG])
                    block.gpsimd(g2a)

                    def g2(v, ec=ec):
                        cu = (ec % NUG) * TT
                        km = 0
                        S.br(v, "dve", s_gtmp[km], s_u[ec % NUG])
                        S.bw(v, "dve", s_u[ec % NUG])
                        tok = S.done(v.tensor_tensor(
                            uB[:, cu:cu + TT], uB[:, cu:cu + TT],
                            gtmp[:, 0:TT],
                            ALU.subtract), "dve")
                        S.mw(tok, s_u[ec % NUG])
                        S.mr(tok, s_gtmp[km])
                    block.vector(g2)

                    def v3(v, ec=ec, i=i):
                        cu = (ec % NUG) * TT
                        kh = ec % 2
                        S.br(v, "dve", s_u[ec % NUG], s_g[ec % NUG],
                             s_carry[ec])
                        S.bw(v, "dve", s_h32[kh])
                        tok = S.done(v.tensor_tensor_scan(
                            h32[:, kh * TT:(kh + 1) * TT],
                            gB[:, cu:cu + TT], uB[:, cu:cu + TT],
                            carry[:, ec:ec + 1],
                            ALU.mult, ALU.add), "dve")
                        S.mw(tok, s_h32[kh])
                        S.mr(tok, s_u[ec % NUG], s_g[ec % NUG], s_carry[ec])
                        S.bw(v, "dve", s_carry[ec])
                        tok2 = S.done(v.tensor_copy(
                            carry[:, ec:ec + 1],
                            h32[:, (kh + 1) * TT - 1:(kh + 1) * TT]), "dve")
                        S.mw(tok2, s_carry[ec])
                        S.mr(tok2, s_h32[kh])
                    block.vector(v3)

                    def g3(g, ec=ec, i=i):
                        kh = ec % 2
                        pr = i % 2
                        S.br(g, "pool", s_h32[kh])
                        S.bw(g, "pool", s_hbf[pr])
                        tok = S.done(g.tensor_copy(
                            hbf[:, pr * DC * TT + ec * TT:
                                pr * DC * TT + (ec + 1) * TT],
                            h32[:, kh * TT:(kh + 1) * TT]), "pool")
                        if ec == DC - 1:
                            S.mw(tok, s_hbf[pr])
                        TK[("A", i, "hbf", ec)] = tok
                        S.mr(tok, s_h32[kh])
                    block.gpsimd(g3)

            def s2a_fills(i):
                pr = i % 2
                for dch in range(2):
                    def pe3(t, dch=dch):
                        S.wait(t, "pe", TK["woT"])
                        for dc in range(dch * 4, dch * 4 + 4):
                            bk = dc % 4
                            S.bw(t, "pe", s_psA[bk])
                            for ec in range(DC):
                                S.wait(t, "pe", TK[("A", i, "hbf", ec)])
                                inst = t.matmul(
                                    psA[bk][:],
                                    woT[:, ec * D + dc * P: ec * D + (dc + 1) * P],
                                    hbf[:, pr * DC * TT + ec * TT:
                                        pr * DC * TT + (ec + 1) * TT],
                                    start=(ec == 0), stop=(ec == DC - 1))
                            tok = S.done(inst, "pe")
                            S.mw(tok, s_psA[bk])
                            S.mr(tok, s_hbf[pr])
                    block.tensor(pe3)

                    def a5(a, dch=dch):
                        for dc in range(dch * 4, dch * 4 + 4):
                            bk = dc % 4
                            S.bw(a, "act", s_aof[dc])
                            S.br(a, "act", s_psA[bk])
                            tok = S.done(a.activation(
                                aof[:, dc * TT:(dc + 1) * TT], psA[bk][:],
                                AF.Identity, bias=bo_s[:, dc:dc + 1]), "act")
                            S.mw(tok, s_aof[dc])
                            S.mr(tok, s_psA[bk])
                    block.scalar(a5)

            def mk_ao_jobs(i):
                jobs = []
                for tc in range(PCT):
                    for half in range(2):
                        def job(tc=tc, half=half):
                            bk = half

                            def pe4(t):
                                S.wait(t, "pe", TK["idf"])
                                S.bw(t, "pe", s_psTF[bk])
                                dcs = range(half * 4, half * 4 + 4)
                                S.br(t, "pe", *[s_aof[dc] for dc in dcs])
                                for j, dc in enumerate(dcs):
                                    inst = t.transpose(
                                        psTF[bk][:, j * P:(j + 1) * P],
                                        aof[:, dc * TT + tc * P:
                                            dc * TT + tc * P + P],
                                        identf[:])
                                tok = S.done(inst, "pe")
                                S.mw(tok, s_psTF[bk])
                                for dc in dcs:
                                    S.mr(tok, s_aof[dc])
                            block.tensor(pe4)

                            def v4(v):
                                S.br(v, "dve", s_psTF[bk])
                                if half == 0:
                                    S.bw(v, "dve", s_aot[tc])
                                tok = S.done(v.tensor_copy(
                                    aot[:, tc * D + half * 512:
                                        tc * D + (half + 1) * 512],
                                    psTF[bk][:]), "dve")
                                if half == 1:
                                    S.mw(tok, s_aot[tc])
                                S.mr(tok, s_psTF[bk])
                            block.vector(v4)
                        jobs.append(job)
                return jobs

            def s2a_rest(i):
                t0 = i * TT
                emit_ln_batch(("A2", i),
                              [(tc, aot[:, tc * D:(tc + 1) * D], [s_aot[tc]])
                               for tc in range(PCT)], grp=1)

                def a6(v):
                    for tc in range(PCT):
                        S.wait(v, "dve", TK[(("A2", i), "ln", tc)])
                        S.bw(v, "dve", s_aot[tc])
                        q = 4 + tc
                        tok = S.done(v.tensor_scalar(
                            aot[:, tc * D:(tc + 1) * D],
                            aot[:, tc * D:(tc + 1) * D],
                            mv[:, q * 2:q * 2 + 1],
                            rstd[:, q:q + 1],
                            ALU.subtract, ALU.mult), "dve")
                        S.mw(tok, s_aot[tc])
                        S.mr(tok, s_stats[q])
                block.vector(a6)

                kxs = [(i * PCT + tc) % NXT for tc in range(PCT)]

                def v5(g):
                    for tc in range(PCT):
                        kx = kxs[tc]
                        S.br(g, "pool", s_aot[tc], s_xt[kx])
                        S.bw(g, "pool", s_xt[kx])
                        tok = S.done(g.tensor_tensor(
                            xt[:, kx * D:(kx + 1) * D],
                            aot[:, tc * D:(tc + 1) * D],
                            xt[:, kx * D:(kx + 1) * D], ALU.add), "pool")
                        S.mw(tok, s_xt[kx])
                        S.mr(tok, s_aot[tc])
                block.gpsimd(v5)

                def a7(a):
                    for tc in range(PCT):
                        kx = kxs[tc]
                        S.br(a, "act", s_xt[kx])
                        tok = S.dma_done(a.dma_start(
                            out=x1s[t0 + tc * P: t0 + (tc + 1) * P, :],
                            in_=xt[:, kx * D:(kx + 1) * D]), f"x1w{tc}")
                        S.mr(tok, s_xt[kx])
                block.scalar(a7)

                emit_ln_batch(("A3", i),
                              [(tc, xt[:, kxs[tc] * D:(kxs[tc] + 1) * D],
                                [s_xt[kxs[tc]]])
                               for tc in range(PCT)], grp=2)

                def v6(v):
                    S.wait(v, "dve",
                           *[TK[(("A3", i), "ln", tc)] for tc in range(PCT)])
                    for tc in range(PCT):
                        pcid = i * PCT + tc
                        q = 8 + tc
                        v.tensor_copy(m3_all[:, pcid:pcid + 1],
                                      mv[:, q * 2:q * 2 + 1])
                        tok = S.done(v.tensor_copy(
                            rs3_all[:, pcid:pcid + 1],
                            rstd[:, q:q + 1]), "dve")
                        TK[("A", i, "st3", tc)] = tok
                        S.mr(tok, s_stats[q])
                block.vector(v6)

                def a8(v):
                    for tc in range(PCT):
                        kx = kxs[tc]
                        S.bw(v, "dve", s_ln3b[tc])
                        S.br(v, "dve", s_xt[kx])
                        S.wait(v, "dve", TK[(("A3", i), "ln", tc)])
                        q = 8 + tc
                        tok = S.done(v.tensor_scalar(
                            ln3b[:, tc * D:(tc + 1) * D],
                            xt[:, kx * D:(kx + 1) * D],
                            mv[:, q * 2:q * 2 + 1],
                            rstd[:, q:q + 1],
                            ALU.subtract, ALU.mult), "dve")
                        S.mw(tok, s_ln3b[tc])
                        S.mr(tok, s_xt[kx], s_stats[q])
                block.vector(a8)

            def stage2b(i):
                t0 = i * TT
                for half in range(2):
                    for sub in range(2):
                        dcg = half * 2 + sub

                        def pe5(t, dcg=dcg, sub=sub):
                            S.bw(t, "pe", s_psTB[sub])
                            for j in range(2):
                                dc = dcg * 2 + j
                                for tc in range(PCT):
                                    S.br(t, "pe", s_ln3b[tc])
                                    inst = t.transpose(
                                        psTB[sub][:, (j * PCT + tc) * P:
                                                   (j * PCT + tc + 1) * P],
                                        ln3b[:, tc * D + dc * P:
                                             tc * D + (dc + 1) * P],
                                        identb[:])
                            tok = S.done(inst, "pe")
                            S.mw(tok, s_psTB[sub])
                            S.mr(tok, *s_ln3b)
                        block.tensor(pe5)

                    def a9(a, half=half, i=i):
                        for sub in range(2):
                            dcg = half * 2 + sub
                            S.bw(a, "act", s_ln3f)
                            S.br(a, "act", s_psTB[sub])
                            tok = S.done(a.copy(
                                ln3f[:, sub * 2 * TT:(sub + 1) * 2 * TT],
                                psTB[sub][:]), "act")
                            TK[("A", i, "l3f", half)] = tok
                            S.mr(tok, s_psTB[sub])
                    block.scalar(a9)

                    def q2(sync, i=i, half=half):
                        S.wait(sync, "sp", TK[("A", i, "l3f", half)])
                        s_ln3f.w = TK[("A", i, "l3f", half)]
                        tok = S.dma_done(sync.dma_start(
                            out=ln3s[half * 4:(half + 1) * 4, :, t0:t0 + TT]
                            .rearrange("dc p t -> p dc t"),
                            in_=ln3f[:].rearrange("p (dc t) -> p dc t", t=TT)),
                            "ln3w")
                        S.mr(tok, s_ln3f)
                    block.sync(q2)

                    def v7(v, i=i, half=half):
                        S.wait(v, "dve", TK[("A", i, "l3f", half)])
                        S.bw(v, "dve", s_tmp1[0])
                        S.br(v, "dve", s_rsum)
                        t1 = S.done(v.reduce_sum(
                            tmp1[:, 0:4],
                            ln3f[:].rearrange("p (dc t) -> p dc t", t=TT),
                            axis=AX.X), "dve")
                        S.wait(v, "dve", t1)
                        t2 = S.done(v.tensor_tensor(
                            rsum[:, half * 4:(half + 1) * 4],
                            rsum[:, half * 4:(half + 1) * 4],
                            tmp1[:, 0:4], ALU.add), "dve")
                        S.mw(t2, s_rsum)
                        S.mr(t2, s_ln3f, s_tmp1[0])
                        TK[("A", i, "rsum")] = t2
                    block.vector(v7)

            stage1a(0)
            stage1a(1)
            stage1(0)
            for i in range(NT):
                s2a_fills(i)
                if i + 2 < NT:
                    stage1a(i + 2)
                if i + 1 < NT:
                    stage1(i + 1, mk_ao_jobs(i))
                else:
                    for job in mk_ao_jobs(i):
                        job()
                if i - 1 >= 0:
                    stage2b(i - 1)
                s2a_rest(i)
            stage2b(NT - 1)

        S.barrier(block)

        with ExitStack() as stB:
            def sbb(name, shape, dt=F32):
                return stB.enter_context(nc.sbuf_tensor(name, shape, dt))
            rlog = sbb("rlog", [1, E]); rnmx = sbb("rnmx", [1, 1])
            rexp = sbb("rexp", [1, E]); rz1 = sbb("rz1", [1, 1])
            rw_t = sbb("rw_t", [1, E]); rmx1 = sbb("rmx1", [1, 1])
            rmx2 = sbb("rmx2", [1, 1]); msk1 = sbb("msk1", [1, E])
            msk2 = sbb("msk2", [1, E]); rtmp = sbb("rtmp", [1, E])
            cw_row = sbb("cw_row", [1, E])
            e2t = sbb("e2t", [1, 1]); z2t = sbb("z2t", [1, 1])
            rz2 = sbb("rz2", [1, 1]); w2t = sbb("w2t", [1, 1])
            io8 = stB.enter_context(nc.sbuf_tensor("io8", [1, E], I32))
            io8f = sbb("io8f", [1, E]); seltmp = sbb("seltmp", [1, E])
            idxf = sbb("idxf", [1, 2])
            idxi = stB.enter_context(nc.sbuf_tensor("idxi", [1, 2], I32))
            cw2row = sbb("cw2row", [1, 2]); cwk = sbb("cwk", [P, 2])

            w1_sb = sbb("w1_sb", [P, DC * F], BF16)
            w2_sb = sbb("w2_sb", [P, FC * D], BF16)
            b1t_sb = sbb("b1t_sb", [P, FC]); b2t_sb = sbb("b2t_sb", [P, DC])
            ln3t = sbb("ln3t", [P, DC * TT], BF16)
            hB = sbb("hB", [P, FC * TT], BF16)
            outB = sbb("outB", [P, DC * TT])

            psRscope = ExitStack()
            psR = psRscope.enter_context(nc.psum_tensor("psR", [1, E], F32))
            psBC = psRscope.enter_context(nc.psum_tensor("psBC", [P, E], F32))

            s_w1 = Slot(); s_w2 = Slot(); s_b12 = Slot()
            s_ln3t = Slot()
            s_hB = [Slot() for _ in range(FC)]
            s_outB = [Slot() for _ in range(DC)]
            s_psH = [Slot() for _ in range(4)]
            s_psO = [Slot() for _ in range(3)]

            def g_io(g):
                TK["io8"] = S.done(g.iota(io8[:], [[1, E]],
                                          channel_multiplier=0), "pool")
            block.gpsimd(g_io)

            def v8(v):
                S.wait(v, "dve", TK[("A", NT - 1, "rsum")])
                tok = S.done(v.tensor_scalar(
                    rsum[:], rsum[:], 1.0 / T, None, ALU.mult), "dve")
                TK["rmean"] = tok
            block.vector(v8)

            def pe6(t):
                S.wait(t, "pe", TK["rmean"])
                for dc in range(DC):
                    inst = t.matmul(
                        psR[:], rsum[:, dc:dc + 1],
                        wrt_s[:, dc * E:(dc + 1) * E],
                        start=(dc == 0), stop=(dc == DC - 1))
                TK["rlogm"] = S.done(inst, "pe")
            block.tensor(pe6)

            def v9(v):
                S.wait(v, "dve", TK["rlogm"])
                t = S.done(v.tensor_tensor(rlog[:], psR[:], br_s[:], ALU.add), "dve")
                S.wait(v, "dve", t)
                t = S.done(v.reduce_max(rnmx[:], rlog[:], axis=AX.X,
                                        negate=True), "dve")
                TK["rnmx"] = t
            block.vector(v9)

            def a10(a):
                S.wait(a, "act", TK["rnmx"])
                TK["rexp"] = S.done(a.activation(
                    rexp[:], rlog[:], AF.Exp, bias=rnmx[:]), "act")
            block.scalar(a10)

            def v10(v):
                S.wait(v, "dve", TK["rexp"])
                t = S.done(v.reduce_sum(rz1[:], rexp[:], axis=AX.X), "dve")
                S.wait(v, "dve", t)
                t = S.done(v.reciprocal(rz1[:], rz1[:]), "dve")
                S.wait(v, "dve", t)
                t = S.done(v.tensor_scalar(
                    rw_t[:], rexp[:], rz1[:], None, ALU.mult), "dve")
                S.wait(v, "dve", t)
                t = S.done(v.reduce_max(rmx1[:], rw_t[:], axis=AX.X), "dve")
                S.wait(v, "dve", t)
                t = S.done(v.tensor_scalar(
                    msk1[:], rw_t[:], rmx1[:], None, ALU.is_equal), "dve")
                S.wait(v, "dve", t)
                t = S.done(v.tensor_scalar(
                    rtmp[:], msk1[:], 1e30, None, ALU.mult), "dve")
                S.wait(v, "dve", t)
                t = S.done(v.tensor_tensor(
                    rtmp[:], rw_t[:], rtmp[:], ALU.subtract), "dve")
                S.wait(v, "dve", t)
                t = S.done(v.reduce_max(rmx2[:], rtmp[:], axis=AX.X), "dve")
                S.wait(v, "dve", t)
                t = S.done(v.tensor_scalar(
                    msk2[:], rw_t[:], rmx2[:], None, ALU.is_equal), "dve")
                S.wait(v, "dve", t)
                t = S.done(v.tensor_tensor(
                    e2t[:], rmx2[:], rmx1[:], ALU.subtract), "dve")
                TK["e2t"] = t
            block.vector(v10)

            def a11(a):
                S.wait(a, "act", TK["e2t"])
                TK["e2e"] = S.done(a.activation(e2t[:], e2t[:], AF.Exp), "act")
            block.scalar(a11)

            def v11(v):
                S.wait(v, "dve", TK["e2e"])
                t = S.done(v.tensor_scalar(
                    z2t[:], e2t[:], 1.0, None, ALU.add), "dve")
                S.wait(v, "dve", t)
                t = S.done(v.reciprocal(rz2[:], z2t[:]), "dve")
                S.wait(v, "dve", t)
                t = S.done(v.tensor_tensor(w2t[:], e2t[:], rz2[:], ALU.mult), "dve")
                S.wait(v, "dve", t)
                S.wait(v, "dve", TK["io8"])
                t = S.done(v.tensor_copy(io8f[:], io8[:]), "dve")
                S.wait(v, "dve", t)
                t = S.done(v.tensor_tensor(
                    seltmp[:], msk1[:], io8f[:], ALU.mult), "dve")
                S.wait(v, "dve", t)
                t = S.done(v.reduce_sum(idxf[:, 0:1], seltmp[:], axis=AX.X), "dve")
                S.wait(v, "dve", t)
                t = S.done(v.tensor_tensor(
                    seltmp[:], msk2[:], io8f[:], ALU.mult), "dve")
                S.wait(v, "dve", t)
                t = S.done(v.reduce_sum(idxf[:, 1:2], seltmp[:], axis=AX.X), "dve")
                S.wait(v, "dve", t)
                TK["idxi"] = S.done(v.tensor_copy(idxi[:], idxf[:]), "dve")
                v.tensor_copy(cw2row[:, 0:1], rz2[:])
                TK["cw2"] = S.done(v.tensor_copy(cw2row[:, 1:2], w2t[:]), "dve")
                t = S.done(v.tensor_scalar(
                    msk1[:], msk1[:], rz2[:], None, ALU.mult), "dve")
                t = S.done(v.tensor_scalar(
                    msk2[:], msk2[:], w2t[:], None, ALU.mult), "dve")
                S.wait(v, "dve", t)
                t = S.done(v.tensor_tensor(
                    cw_row[:], msk1[:], msk2[:], ALU.add), "dve")
                TK["cw"] = t
            block.vector(v11)

            def pe7(t):
                S.wait(t, "pe", TK["cw"], TK["cw2"])
                TK["cwm"] = S.done(t.matmul(
                    psBC[:], ones1f[:], cw_row[:], start=True, stop=True), "pe")
            block.tensor(pe7)

            def v12(v):
                S.wait(v, "dve", TK["cwm"])
                TK["cwb"] = S.done(v.tensor_copy(cwb[:], psBC[:]), "dve")
            block.vector(v12)

            def pe7b(t):
                S.wait(t, "pe", TK["cwb"])
                TK["cwm2"] = S.done(t.matmul(
                    psBC[:, 0:2], ones1f[:], cw2row[:], start=True, stop=True),
                    "pe")
            block.tensor(pe7b)

            def v12b(v):
                S.wait(v, "dve", TK["cwm2"])
                TK["cwk"] = S.done(v.tensor_copy(cwk[:], psBC[:, 0:2]), "dve")
            block.vector(v12b)

            psRscope.close()
            psH = [stB.enter_context(nc.psum_tensor(f"psH{i}", [P, TT], F32))
                   for i in range(4)]
            psO = [stB.enter_context(nc.psum_tensor(f"psO{i}", [P, TT], F32))
                   for i in range(3)]

            NEXP = 2 if SPARSE else E
            REGS = {}
            for e in range(NEXP):
                if SPARSE:
                    def gw(g, e=e):
                        if "r_idx" not in REGS:
                            REGS["r_idx"] = g.alloc_register("r_idx")
                            REGS["r_off"] = g.alloc_register("r_off")
                        r_idx, r_off = REGS["r_idx"], REGS["r_off"]
                        S.wait(g, "pool", TK["idxi"])
                        g.reg_load(r_idx, idxi[0:1, e:e + 1])
                        S.bw(g, "pool", s_w1)
                        for j in range(2):
                            g.reg_mul(r_off, r_idx, D * F)
                            g.reg_add(r_off, r_off, j * 4 * P * F)
                            tok = S.dma_done(g.dma_start(
                                out=w1_sb[:, j * 4 * F:(j + 1) * 4 * F]
                                .rearrange("p (dc f) -> p dc f", f=F),
                                in_=bass.AP(dh["W1"], r_off,
                                            [[F, P], [P * F, 4], [1, F]])),
                                "w1")
                        S.mw(tok, s_w1)
                        TK[("B", e, "w1")] = tok
                        S.bw(g, "pool", s_w2)
                        for j in range(2):
                            g.reg_mul(r_off, r_idx, F * D)
                            g.reg_add(r_off, r_off, j * 16 * P * D)
                            tok = S.dma_done(g.dma_start(
                                out=w2_sb[:, j * 16 * D:(j + 1) * 16 * D]
                                .rearrange("p (fc d) -> p fc d", d=D),
                                in_=bass.AP(dh["W2"], r_off,
                                            [[D, P], [P * D, 16], [1, D]])),
                                "w2")
                        S.mw(tok, s_w2)
                        TK[("B", e, "w2")] = tok
                        S.bw(g, "pool", s_b12)
                        g.reg_mul(r_off, r_idx, P * FC)
                        S.dma_done(g.dma_start(
                            out=b1t_sb[:],
                            in_=bass.AP(dh["b1t"], r_off, [[FC, P], [1, FC]])),
                            "bld")
                        g.reg_mul(r_off, r_idx, P * DC)
                        tok = S.dma_done(g.dma_start(
                            out=b2t_sb[:],
                            in_=bass.AP(dh["b2t"], r_off, [[DC, P], [1, DC]])),
                            "bld")
                        S.mw(tok, s_b12)
                        TK[("B", e, "b12")] = tok
                    block.gpsimd(gw)
                else:
                    def gw(g, e=e):
                        S.bw(g, "pool", s_w1)
                        for dc in range(DC):
                            tok = S.dma_done(g.dma_start(
                                out=w1_sb[:, dc * F:(dc + 1) * F],
                                in_=d["W1"][e, dc * P:(dc + 1) * P, :]), "w1")
                        S.mw(tok, s_w1)
                        TK[("B", e, "w1")] = tok
                        S.bw(g, "pool", s_w2)
                        for fc in range(FC):
                            tok = S.dma_done(g.dma_start(
                                out=w2_sb[:, fc * D:(fc + 1) * D],
                                in_=d["W2"][e, fc * P:(fc + 1) * P, :]), "w2")
                        S.mw(tok, s_w2)
                        TK[("B", e, "w2")] = tok
                    block.gpsimd(gw)

                    def qb(sync, e=e):
                        S.bw(sync, "sp", s_b12)
                        S.dma_done(sync.dma_start(
                            out=b1t_sb[:], in_=d["b1t"][e]), "bld")
                        tok = S.dma_done(sync.dma_start(
                            out=b2t_sb[:], in_=d["b2t"][e]), "bld")
                        S.mw(tok, s_b12)
                        TK[("B", e, "b12")] = tok
                    block.sync(qb)

                for i in range(NT):
                    t0 = i * TT

                    def q3(sync, i=i, e=e):
                        S.bw(sync, "sp", s_ln3t)
                        tok = S.dma_done(sync.dma_start(
                            out=ln3t[:].rearrange("p (dc t) -> p dc t", t=TT),
                            in_=ln3s[:, :, t0:t0 + TT].rearrange(
                                "dc p t -> p dc t")), "ln3r")
                        S.mw(tok, s_ln3t)
                    block.sync(q3)

                    for fcg in range(FC // 4):
                        def pe8(t, fcg=fcg, e=e):
                            S.wait(t, "pe", TK[("B", e, "w1")], TK["cwk"])
                            S.br(t, "pe", s_ln3t)
                            for fc in range(fcg * 4, fcg * 4 + 4):
                                bk = fc % 4
                                S.bw(t, "pe", s_psH[bk])
                                for dc in range(DC):
                                    inst = t.matmul(
                                        psH[bk][:],
                                        w1_sb[:, dc * F + fc * P:
                                              dc * F + (fc + 1) * P],
                                        ln3t[:, dc * TT:(dc + 1) * TT],
                                        start=(dc == 0), stop=(dc == DC - 1))
                                tok = S.done(inst, "pe")
                                S.mw(tok, s_psH[bk])
                                S.mr(tok, s_ln3t, s_w1)
                        block.tensor(pe8)

                        def a12(a, fcg=fcg, e=e):
                            S.wait(a, "act", TK[("B", e, "b12")])
                            for fc in range(fcg * 4, fcg * 4 + 4):
                                bk = fc % 4
                                S.bw(a, "act", s_hB[fc])
                                S.br(a, "act", s_psH[bk])
                                tok = S.done(a.activation(
                                    hB[:, fc * TT:(fc + 1) * TT], psH[bk][:],
                                    AF.Gelu, bias=b1t_sb[:, fc:fc + 1]), "act")
                                S.mw(tok, s_hB[fc])
                                S.mr(tok, s_psH[bk], s_b12)
                        block.scalar(a12)

                    for dcg in ((0, 1, 2), (3, 4, 5), (6, 7)):
                        def pe9(t, dcg=dcg, e=e):
                            S.wait(t, "pe", TK[("B", e, "w2")])
                            S.br(t, "pe", *s_hB)
                            for dc in dcg:
                                bk = dc % 3
                                S.bw(t, "pe", s_psO[bk])
                                for fc in range(FC):
                                    inst = t.matmul(
                                        psO[bk][:],
                                        w2_sb[:, fc * D + dc * P:
                                              fc * D + (dc + 1) * P],
                                        hB[:, fc * TT:(fc + 1) * TT],
                                        start=(fc == 0), stop=(fc == FC - 1))
                                tok = S.done(inst, "pe")
                                S.mw(tok, s_psO[bk])
                                S.mr(tok, s_w2, *s_hB)
                        block.tensor(pe9)

                        def v13(v, dcg=dcg, e=e):
                            S.wait(v, "dve", TK["cwb"], TK["cwk"],
                                   TK[("B", e, "b12")])
                            for dc in dcg:
                                bk = dc % 3
                                S.bw(v, "dve", s_outB[dc])
                                S.br(v, "dve", s_psO[bk])
                                scale_col = (cwk[:, e:e + 1] if SPARSE
                                             else cwb[:, e:e + 1])
                                tok = S.done(v.tensor_scalar(
                                    outB[:, dc * TT:(dc + 1) * TT], psO[bk][:],
                                    b2t_sb[:, dc:dc + 1], scale_col,
                                    ALU.add, ALU.mult), "dve")
                                S.mw(tok, s_outB[dc])
                                S.mr(tok, s_psO[bk], s_b12)
                        block.vector(v13)

                    def q4(g, i=i, e=e):
                        S.br(g, "pool", *s_outB)
                        S.wait(g, "pool", ("q_accw", S.cnt["q_accw"]))
                        tok = S.dma_done(g.dma_start(
                            out=accs[:, :, t0:t0 + TT].rearrange(
                                "dc p t -> p dc t"),
                            in_=outB[:].rearrange("p (dc t) -> p dc t", t=TT),
                            accum_op=(ALU.bypass if e == 0 else ALU.add)),
                            "accw")
                        for sl in s_outB:
                            S.mr(tok, sl)
                        TK[("B", e, i, "acc")] = tok
                    block.gpsimd(q4)

        S.barrier(block)

        with ExitStack() as stC:
            def sbc(name, shape, dt=F32):
                return stC.enter_context(nc.sbuf_tensor(name, shape, dt))
            gam_r = sbc("gam_r", [1, D]); bet_r = sbc("bet_r", [1, D])
            gam_b = sbc("gam_b", [P, D]); bet_b = sbc("bet_b", [P, D])
            accf = sbc("accf", [P, 2 * DC * TT])
            x1c = sbc("x1c", [P, 3 * 4 * D])
            yt = sbc("yt", [P, 3 * 4 * D])
            zt = sbc("zt", [P, 4 * D])
            ot = sbc("ot", [P, 4 * D])
            tmpc = sbc("tmpc", [P, 4 * D])
            st6c = sbc("st6c", [P, 3 * 48])
            mvc = sbc("mvc", [P, 3 * 8]); sdc = sbc("sdc", [P, 3 * 4])
            rsc = sbc("rsc", [P, 3 * 4])

            psC = [stC.enter_context(nc.psum_tensor(f"psC{i}", [P, 4 * P], F32))
                   for i in range(2)]
            psG = stC.enter_context(nc.psum_tensor("psG", [P, D], F32))

            s_accf = [Slot() for _ in range(2)]
            s_x1c = [Slot() for _ in range(3)]
            s_yt = [Slot() for _ in range(3)]
            s_zt = [Slot()]
            s_zt.append(s_zt[0])
            s_ot = [Slot()]
            s_ot.append(s_ot[0])
            s_tmpc = [Slot()]
            s_psC = [Slot() for _ in range(2)]
            s_statc = [Slot() for _ in range(12)]

            def qc(sync):
                S.dma_done(sync.dma_start(out=gam_r[:], in_=d["gamma"][:]),
                           "cst")
                tok = S.dma_done(sync.dma_start(out=bet_r[:], in_=d["beta"][:]),
                                 "cst")
                TK["gb"] = tok
            block.sync(qc)

            def pec(t):
                S.wait(t, "pe", TK["gb"])
                t.matmul(psG[:, 0:512], ones1f[:], gam_r[:, 0:512],
                         start=True, stop=True)
                TK["gbm"] = S.done(t.matmul(
                    psG[:, 512:1024], ones1f[:], gam_r[:, 512:1024],
                    start=True, stop=True), "pe")
            block.tensor(pec)

            def vc(v):
                S.wait(v, "dve", TK["gbm"])
                TK["gbe"] = S.done(v.tensor_copy(gam_b[:], psG[:]), "dve")
            block.vector(vc)

            def pec2(t):
                S.wait(t, "pe", TK["gbe"])
                t.matmul(psG[:, 0:512], ones1f[:], bet_r[:, 0:512],
                         start=True, stop=True)
                TK["bbm"] = S.done(t.matmul(
                    psG[:, 512:1024], ones1f[:], bet_r[:, 512:1024],
                    start=True, stop=True), "pe")
            block.tensor(pec2)

            def vc2(v):
                S.wait(v, "dve", TK["bbm"])
                TK["bbe"] = S.done(v.tensor_copy(bet_b[:], psG[:]), "dve")
            block.vector(vc2)

            def emit_ln_c(tag, mk_src, grp):
                def v1(v):
                    for tc in range(PCT):
                        src_ap, slots = mk_src(tc)
                        q = grp * 4 + tc
                        S.bw(v, "dve", s_statc[q])
                        S.br(v, "dve", *slots)
                        v.bn_stats(st6c[:, q * 12:q * 12 + 6],
                                   src_ap[:, 0:512])
                        t = S.done(v.bn_stats(
                            st6c[:, q * 12 + 6:q * 12 + 12],
                            src_ap[:, 512:1024]), "dve")
                        S.wait(v, "dve", t)
                        TK[(tag, "mv", tc)] = S.done(v.bn_aggr(
                            mvc[:, q * 2:q * 2 + 2],
                            st6c[:, q * 12:(q + 1) * 12]), "dve")
                block.vector(v1)

                def a1(a):
                    S.wait(a, "act",
                           *[TK[(tag, "mv", tc)] for tc in range(PCT)])
                    for tc in range(PCT):
                        q = grp * 4 + tc
                        tok = S.done(a.activation(
                            sdc[:, q:q + 1], mvc[:, q * 2 + 1:q * 2 + 2],
                            AF.Sqrt, bias=eps_t[:]), "act")
                    TK[(tag, "sd")] = tok
                block.scalar(a1)

                def v2(v):
                    S.wait(v, "dve", TK[(tag, "sd")])
                    for tc in range(PCT):
                        q = grp * 4 + tc
                        tok = S.done(v.reciprocal(
                            rsc[:, q:q + 1], sdc[:, q:q + 1]), "dve")
                        S.mw(tok, s_statc[q])
                        _, slots = mk_src(tc)
                        for ss in slots:
                            S.mr(tok, ss)
                        TK[(tag, "ln", tc)] = tok
                block.vector(v2)

            def stageC1(i):
                t0 = i * TT
                ki = i % 2
                ky = i % 3

                def q5(sync, i=i, ki=ki, t0=t0):
                    S.bw(sync, "sp", s_accf[ki])
                    tok = S.dma_done(sync.dma_start(
                        out=accf[:, ki * DC * TT:(ki + 1) * DC * TT].rearrange(
                            "p (dc t) -> p dc t", t=TT),
                        in_=accs[:, :, t0:t0 + TT].rearrange("dc p t -> p dc t")),
                        f"acc{ki}")
                    S.mw(tok, s_accf[ki])
                    kx = i % 3
                    S.bw(sync, "sp", s_x1c[kx])
                    tok = S.dma_done(sync.dma_start(
                        out=x1c[:, kx * 4 * D:(kx + 1) * 4 * D].rearrange(
                            "p (tc d) -> p tc d", d=D),
                        in_=x1s.rearrange("(n tc p) d -> n p tc d",
                                          p=P, tc=PCT)[i]), f"x1c{kx}")
                    S.mw(tok, s_x1c[kx])
                block.sync(q5)

                for tc in range(PCT):
                    for half in range(2):
                        bk = half

                        def pe10(t, i=i, ki=ki, tc=tc, half=half, bk=bk):
                            S.wait(t, "pe", TK["idf"])
                            S.bw(t, "pe", s_psC[bk])
                            S.br(t, "pe", s_accf[ki])
                            for j in range(4):
                                dc = half * 4 + j
                                inst = t.transpose(
                                    psC[bk][:, j * P:(j + 1) * P],
                                    accf[:, ki * DC * TT + dc * TT + tc * P:
                                         ki * DC * TT + dc * TT + tc * P + P],
                                    identf[:])
                            tok = S.done(inst, "pe")
                            S.mw(tok, s_psC[bk])
                            S.mr(tok, s_accf[ki])
                        block.tensor(pe10)

                    def ac0(a, ky=ky, tc=tc):
                        if tc == 0:
                            S.bw(a, "act", s_yt[ky])
                        for half in range(2):
                            bk = half
                            S.br(a, "act", s_psC[bk])
                            tok = S.done(a.copy(
                                yt[:, ky * 4 * D + tc * D + half * 512:
                                   ky * 4 * D + tc * D + (half + 1) * 512],
                                psC[bk][:]), "act")
                            S.mr(tok, s_psC[bk])
                        if tc == PCT - 1:
                            S.mw(tok, s_yt[ky])
                    block.scalar(ac0)

            def stageC2a(i):
                ki = i % 2
                kx = i % 3
                ky = i % 3
                def vc4(v, i=i, ki=ki, kx=kx):
                    S.bw(v, "dve", s_tmpc[0])
                    S.br(v, "dve", s_x1c[kx])
                    for tc in range(PCT):
                        pcid = i * PCT + tc
                        S.wait(v, "dve", TK[("A", i, "st3", tc)])
                        tok = S.done(v.tensor_scalar(
                            tmpc[:, tc * D:(tc + 1) * D],
                            x1c[:, kx * 4 * D + tc * D:
                                kx * 4 * D + (tc + 1) * D],
                            m3_all[:, pcid:pcid + 1],
                            rs3_all[:, pcid:pcid + 1],
                            ALU.subtract, ALU.mult), "dve")
                    S.mw(tok, s_tmpc[0])
                    S.mr(tok, s_x1c[kx])
                block.vector(vc4)

                def gc1(v, ky=ky):
                    S.br(v, "dve", s_tmpc[0], s_yt[ky])
                    S.bw(v, "dve", s_yt[ky])
                    for tc in range(PCT):
                        tok = S.done(v.tensor_tensor(
                            yt[:, ky * 4 * D + tc * D:
                               ky * 4 * D + (tc + 1) * D],
                            yt[:, ky * 4 * D + tc * D:
                               ky * 4 * D + (tc + 1) * D],
                            tmpc[:, tc * D:(tc + 1) * D], ALU.add), "dve")
                    S.mw(tok, s_yt[ky])
                    S.mr(tok, s_tmpc[0])
                block.vector(gc1)

                emit_ln_c(("Cy", i),
                          lambda tc, ky=ky: (
                              yt[:, ky * 4 * D + tc * D:
                                 ky * 4 * D + (tc + 1) * D], [s_yt[ky]]),
                          grp=ki)

            def stageC2b(i):
                ki = i % 2
                ky = i % 3

                def vc5(v, i=i, ki=ki, ky=ky):
                    S.bw(v, "dve", s_zt[ki])
                    S.br(v, "dve", s_yt[ky])
                    for tc in range(PCT):
                        q = ki * 4 + tc
                        S.wait(v, "dve", TK[(("Cy", i), "ln", tc)])
                        tok = S.done(v.tensor_scalar(
                            zt[:, tc * D:(tc + 1) * D],
                            yt[:, ky * 4 * D + tc * D:
                               ky * 4 * D + (tc + 1) * D],
                            mvc[:, q * 2:q * 2 + 1],
                            rsc[:, q:q + 1],
                            ALU.subtract, ALU.mult), "dve")
                        S.mr(tok, s_statc[q])
                    S.mw(tok, s_zt[ki])
                    S.mr(tok, s_yt[ky])
                block.vector(vc5)

                def gc2(v, ki=ki):
                    S.wait(v, "dve", TK["gbe"], TK["bbe"])
                    S.br(v, "dve", s_zt[ki])
                    S.bw(v, "dve", s_zt[ki])
                    for tc in range(PCT):
                        t1 = S.done(v.tensor_tensor(
                            zt[:, tc * D:(tc + 1) * D],
                            zt[:, tc * D:(tc + 1) * D],
                            gam_b[:], ALU.mult), "dve")
                        S.wait(v, "dve", t1)
                        tok = S.done(v.tensor_tensor(
                            zt[:, tc * D:(tc + 1) * D],
                            zt[:, tc * D:(tc + 1) * D],
                            bet_b[:], ALU.add), "dve")
                    S.mw(tok, s_zt[ki])
                block.vector(gc2)

                emit_ln_c(("Cz", i),
                          lambda tc: (
                              zt[:, tc * D:(tc + 1) * D], [s_zt[0]]),
                          grp=2)

            def stageC2c(i):
                t0 = i * TT
                ki = i % 2
                kx = i % 3
                def vc7(v, i=i, ki=ki):
                    S.bw(v, "dve", s_ot[ki])
                    S.br(v, "dve", s_zt[ki])
                    for tc in range(PCT):
                        q = 2 * 4 + tc
                        S.wait(v, "dve", TK[(("Cz", i), "ln", tc)])
                        tok = S.done(v.tensor_scalar(
                            ot[:, tc * D:(tc + 1) * D],
                            zt[:, tc * D:(tc + 1) * D],
                            mvc[:, q * 2:q * 2 + 1],
                            rsc[:, q:q + 1],
                            ALU.subtract, ALU.mult), "dve")
                        S.mr(tok, s_statc[q])
                    S.mw(tok, s_ot[ki])
                    S.mr(tok, s_zt[ki])
                block.vector(vc7)

                def gc3(g, ki=ki, kx=kx):
                    S.br(g, "pool", s_ot[ki], s_x1c[kx])
                    S.bw(g, "pool", s_ot[ki])
                    for tc in range(PCT):
                        tok = S.done(g.tensor_tensor(
                            ot[:, tc * D:(tc + 1) * D],
                            ot[:, tc * D:(tc + 1) * D],
                            x1c[:, kx * 4 * D + tc * D:
                                kx * 4 * D + (tc + 1) * D], ALU.add), "pool")
                    S.mw(tok, s_ot[ki])
                    S.mr(tok, s_x1c[kx])
                block.gpsimd(gc3)

                def ac3(a, i=i, ki=ki, t0=t0):
                    S.br(a, "act", s_ot[ki])
                    for tc in range(PCT):
                        tok = S.dma_done(a.dma_start(
                            out=out_d[t0 + tc * P: t0 + (tc + 1) * P, :],
                            in_=ot[:, tc * D:(tc + 1) * D]),
                            f"outw{ki}")
                        S.mr(tok, s_ot[ki])
                block.scalar(ac3)

            for k in range(NT + 3):
                if 3 <= k:
                    stageC2c(k - 3)
                if 2 <= k < NT + 2:
                    stageC2b(k - 2)
                if 1 <= k < NT + 1:
                    stageC2a(k - 1)
                if k < NT:
                    stageC1(k)

            def qf(sync):
                S.wait(sync, "sp", ("q_outw0", S.cnt["q_outw0"]),
                       ("q_outw1", S.cnt["q_outw1"]))
            block.sync(qf)

    nc.finalize()
    return nc


def kernel(L0_attn_x, L0_attn_Wi, L0_attn_bi, L0_attn_Wg, L0_attn_bg,
           L0_attn_Wo, L0_attn_bo, L0_ffn_W1, L0_ffn_b1, L0_ffn_W2,
           L0_ffn_b2, L0_ffn_Wr, L0_ffn_br, L0_ffn_gamma, L0_ffn_beta):
    x = np.asarray(L0_attn_x, dtype=np.float32)
    B, T, _ = x.shape
    fm = lambda v: np.ascontiguousarray(
        np.asarray(v, dtype=np.float32).reshape(DC, P).T)
    common = {
        "WiT": np.ascontiguousarray(np.asarray(L0_attn_Wi, np.float32).T),
        "WgT": np.ascontiguousarray(np.asarray(L0_attn_Wg, np.float32).T),
        "WoT": np.ascontiguousarray(np.asarray(L0_attn_Wo, np.float32).T),
        "bi": fm(L0_attn_bi), "bg": fm(L0_attn_bg), "bo": fm(L0_attn_bo),
        "W1": np.ascontiguousarray(np.asarray(L0_ffn_W1, np.float32)),
        "b1t": np.ascontiguousarray(
            np.asarray(L0_ffn_b1, np.float32).reshape(E, FC, P)
            .transpose(0, 2, 1)),
        "W2": np.ascontiguousarray(np.asarray(L0_ffn_W2, np.float32)),
        "b2t": np.ascontiguousarray(
            np.asarray(L0_ffn_b2, np.float32).reshape(E, DC, P)
            .transpose(0, 2, 1)),
        "Wrt": np.ascontiguousarray(
            np.asarray(L0_ffn_Wr, np.float32).reshape(DC, P, E)
            .transpose(1, 0, 2)),
        "br": np.asarray(L0_ffn_br, np.float32).reshape(1, E),
        "gamma": np.asarray(L0_ffn_gamma, np.float32).reshape(1, D),
        "beta": np.asarray(L0_ffn_beta, np.float32).reshape(1, D),
    }
    nc = build(T)
    in_maps = [dict(common, x=np.ascontiguousarray(x[b])) for b in range(B)]
    res = run_bass_kernel_spmd(nc, in_maps, core_ids=list(range(B)))
    return np.stack([np.asarray(r["out"], dtype=np.float32)
                     for r in res.results], axis=0)
